# version 13
# baseline (speedup 1.0000x reference)
import math
from contextlib import ExitStack

import numpy as np

import concourse.bass as bass
import concourse.mybir as mybir
from concourse.bass_utils import run_bass_kernel_spmd

F32 = mybir.dt.float32
BF16 = mybir.dt.bfloat16
I32 = mybir.dt.int32
AF = mybir.ActivationFunctionType
ALU = mybir.AluOpType
AX = mybir.AxisListType

NCORES = 8
S = 2048
D = 2048
NT = S // 128
ND = D // 128
H = 16
KVC, QC = 512, 1536
NCC, NQC = KVC // 128, QC // 128
E, HF = 8, 1408
NHC = HF // 128
NR = E + 2
NTG = S // 512
SCALE = 1.0 / math.sqrt(128.0)
EPS = 1e-5


def build_nc():
    nc = bass.Bass("TRN2", target_bir_lowering=False)

    xin = nc.dram_tensor("xin", [S, D], F32, kind="ExternalInput")
    xTd = nc.dram_tensor("xTd", [D, S], F32, kind="ExternalInput")
    dkvT = nc.dram_tensor("dkvT", [NCC, 128, 2048], F32, kind="ExternalInput")
    dqT = nc.dram_tensor("dqT", [NQC, 128, 2048], F32, kind="ExternalInput")
    ukT = nc.dram_tensor("ukT", [H, 128, 512], F32, kind="ExternalInput")
    uvT = nc.dram_tensor("uvT", [H, 128, 512], F32, kind="ExternalInput")
    uqT = nc.dram_tensor("uqT", [H, 128, 1536], F32, kind="ExternalInput")
    woT = nc.dram_tensor("woT", [ND, 128, 2048], F32, kind="ExternalInput")
    gwp = nc.dram_tensor("gwp", [128, 128], F32, kind="ExternalInput")
    wgp = nc.dram_tensor("wgp", [E, NHC, 128, 2048], F32, kind="ExternalInput")
    wup = nc.dram_tensor("wup", [E, NHC, 128, 2048], F32, kind="ExternalInput")
    wdp = nc.dram_tensor("wdp", [E, NHC, 128, 2048], F32, kind="ExternalInput")
    swgp = nc.dram_tensor("swgp", [22, 128, 2048], F32, kind="ExternalInput")
    swup = nc.dram_tensor("swup", [22, 128, 2048], F32, kind="ExternalInput")
    swdp = nc.dram_tensor("swdp", [22, 128, 2048], F32, kind="ExternalInput")
    dkvb = nc.dram_tensor("dkvb", [128, NCC], F32, kind="ExternalInput")
    dqb = nc.dram_tensor("dqb", [128, NQC], F32, kind="ExternalInput")
    ukb = nc.dram_tensor("ukb", [128, H], F32, kind="ExternalInput")
    uqb = nc.dram_tensor("uqb", [128, H], F32, kind="ExternalInput")
    wob = nc.dram_tensor("wob", [128, ND], F32, kind="ExternalInput")
    uvbr = nc.dram_tensor("uvbr", [1, 2048], F32, kind="ExternalInput")

    outT = nc.dram_tensor("outT", [D, S], F32, kind="ExternalOutput")
    partial = nc.dram_tensor("partial", [D, S], F32, kind="Internal")
    combhbm = nc.dram_tensor("combhbm", [8, 2048], BF16, kind="Internal")

    ident = nc.alloc_sbuf_tensor("ident", [128, 128], BF16)
    onesc = nc.alloc_sbuf_tensor("onesc", [1, 128], BF16)
    gwv = nc.alloc_sbuf_tensor("gwv", [128, 128], BF16)
    uvb_bc = nc.alloc_sbuf_tensor("uvb_bc", [128, 2048], BF16)
    combT = nc.alloc_sbuf_tensor("combT", [8, S], BF16)
    b_dkv = nc.alloc_sbuf_tensor("b_dkv", [128, NCC], F32)
    b_dq = nc.alloc_sbuf_tensor("b_dq", [128, NQC], F32)
    b_uk = nc.alloc_sbuf_tensor("b_uk", [128, H], F32)
    b_uq = nc.alloc_sbuf_tensor("b_uq", [128, H], F32)
    b_wo = nc.alloc_sbuf_tensor("b_wo", [128, ND], F32)
    stat = [nc.alloc_sbuf_tensor(f"stat{p}", [128, 16], F32) for p in range(2)]
    ssq = [nc.alloc_sbuf_tensor(f"ssq{p}", [128, 1], F32) for p in range(2)]
    sums = [nc.alloc_sbuf_tensor(f"sums{p}", [128, 1], F32) for p in range(2)]
    sd = [nc.alloc_sbuf_tensor(f"sd{p}", [128, 1], F32) for p in range(2)]
    negm = [nc.alloc_sbuf_tensor(f"negm{p}", [128, 1], F32) for p in range(2)]
    sume = [nc.alloc_sbuf_tensor(f"sume{p}", [128, 1], F32) for p in range(2)]
    prb = [nc.alloc_sbuf_tensor(f"prb{p}", [128, 8], F32) for p in range(2)]
    cmb = [nc.alloc_sbuf_tensor(f"cmb{p}", [128, 8], BF16) for p in range(2)]
    lgw = nc.alloc_sbuf_tensor("lgw", [128, 8], F32)
    negb = nc.alloc_sbuf_tensor("negb", [128, 1], F32)
    negA = nc.alloc_sbuf_tensor("negA", [128, NT], F32)
    inv = nc.alloc_sbuf_tensor("inv", [128, 1], F32)

    xnT = nc.alloc_sbuf_tensor("xnT", [128, ND * S], BF16)

    A = nc.alloc_psum_tensor("A", [128, 2048], F32)
    B = nc.alloc_psum_tensor("B", [128, 2048], F32)

    sem_names = [
        "gI", "vI",
        "sSq", "sSd", "sXn", "vX", "vVar", "vSt", "tTr", "vTr",
        "tLg", "vLa", "sLx", "vCb", "tCT", "vCT", "vW",
        "tGa", "tGb", "sSl", "vT1", "vT2",
        "tCb", "vCbc", "tDn", "vDn",
        "tPj", "sPj",
        "gOn", "tKQ", "sKQ", "vV", "tS", "sEx", "tAV", "vAv", "tAT", "vAT",
        "tWo", "vWo",
    ]
    SM = {n: nc.alloc_semaphore(n) for n in sem_names}
    vw_count = [0]

    class DmaStream:

        def __init__(self, name, nslots):
            self.sems = [nc.alloc_semaphore(f"{name}_{i}") for i in range(nslots)]
            self.tot = [0] * nslots
            self.mark = {}

        def issue(self, inst, slot, label=None):
            k = slot % len(self.sems)
            inst.then_inc(self.sems[k], 16)
            self.tot[k] += 16
            if label is not None:
                self.mark[label] = (self.sems[k], self.tot[k])

        def wait(self, eng, label):
            s, v = self.mark[label]
            eng.wait_ge(s, v)

    DS = {
        "dB": DmaStream("dB", 1), "dG": DmaStream("dG", 1),
        "dX": DmaStream("dX", 2),
        "dWg": DmaStream("dWg", 2), "dWd": DmaStream("dWd", 1),
        "dCw": DmaStream("dCw", 1), "dCb": DmaStream("dCb", 1),
        "dRr": DmaStream("dRr", 3), "dRw": DmaStream("dRw", 4),
        "dPw": DmaStream("dPw", 2), "dQw": DmaStream("dQw", 2), "dHw": DmaStream("dHw", 2),
        "dWo": DmaStream("dWo", 16),
        "dPr": DmaStream("dPr", 2), "dXr": DmaStream("dXr", 2),
        "dOw": DmaStream("dOw", 2),
    }

    def vi(inst):
        inst.then_inc(SM["vW"], 1)
        vw_count[0] += 1
        return inst

    xnT3 = xnT.ap().rearrange("p (dc t) -> p dc t", dc=ND)

    st = ExitStack()
    gw = [st.enter_context(nc.sbuf_tensor(f"gw{p}", [128, 2048], BF16, side="right"))
          for p in range(2)]
    uw = [st.enter_context(nc.sbuf_tensor(f"uw{p}", [128, 2048], BF16, side="right"))
          for p in range(2)]
    sgb = st.enter_context(nc.sbuf_tensor("sgb", [128, 2048], BF16, side="right"))
    upb = st.enter_context(nc.sbuf_tensor("upb", [128, 2048], BF16, side="right"))
    comb_bc = st.enter_context(nc.sbuf_tensor("comb_bc", [128, 2048], BF16, side="right"))
    gpr = st.enter_context(nc.sbuf_tensor("gpr", [128, NHC * 2048], BF16, side="right"))
    wdb = st.enter_context(nc.sbuf_tensor("wdb", [128, NHC * 2048], BF16, side="right"))
    rmwb = [st.enter_context(nc.sbuf_tensor(f"rmwb{p}", [128, 512], F32, side="right"))
            for p in range(3)]
    outb = [st.enter_context(nc.sbuf_tensor(f"outb{p}", [128, 512], F32, side="right"))
            for p in range(4)]
    wdb_addr = nc.lookup_mloc(wdb).addr
    xt = [nc.alloc_sbuf_tensor_at(f"xt{p}", [128, 2048], F32, offset=wdb_addr + p * 8192)
          for p in range(2)]
    xnm = [nc.alloc_sbuf_tensor_at(f"xnm{p}", [128, 2048], BF16,
                                   offset=wdb_addr + 16384 + p * 4096)
           for p in range(2)]
    sgb_addr = nc.lookup_mloc(sgb).addr
    io1 = nc.alloc_sbuf_tensor_at("io1", [128, 128], F32, offset=sgb_addr)
    ioc = nc.alloc_sbuf_tensor_at("ioc", [128, 1], F32, offset=sgb_addr + 512)

    def lgp_of(tt):
        return A[:, 1024 + 8 * tt: 1024 + 8 * (tt + 1)]

    def trp_of(G):
        return (A[:, 0:256] if G % 2 == 0 else A[:, 512:768]).bitcast(BF16)

    def ctp_of(p):
        return (B[:, 0:64] if p == 0 else B[:, 512:576]).bitcast(BF16)

    gps = [A[:, 0:512], A[:, 512:1024]]
    ups = [A[:, 1024:1536], A[:, 1536:2048]]
    dps = [B[:, 0:512], B[:, 512:1024]]

    def wsrc(r):
        if r < E:
            return (lambda hc: wgp.ap()[r, hc], lambda hc: wup.ap()[r, hc],
                    lambda hc: wdp.ap()[r, hc])
        half = r - E
        return (lambda hc: swgp.ap()[half * NHC + hc], lambda hc: swup.ap()[half * NHC + hc],
                lambda hc: swdp.ap()[half * NHC + hc])

    with nc.Block() as blk:
        @blk.gpsimd
        def _(g):
            g.memset(onesc[:, :], 1.0).then_inc(SM["gI"], 1)
            g.iota(io1[:, :], [[1, 128]], base=0, channel_multiplier=0,
                   allow_small_or_imprecise_dtypes=True).then_inc(SM["gI"], 1)
            g.iota(ioc[:, :], [[0, 1]], base=0, channel_multiplier=1,
                   allow_small_or_imprecise_dtypes=True).then_inc(SM["gI"], 1)
            DS["dG"].issue(g.dma_start(gwv[:, :], gwp.ap()), 0)
            DS["dG"].issue(g.dma_start(uvb_bc[:, :],
                                       uvbr.ap().partition_broadcast(128)[:, :]), 0, "setup")
            for r in range(NR):
                gsrc, usrc, dsrc = wsrc(r)
                if r < E:
                    if r == 0:
                        g.wait_ge(SM["vCT"], NT)
                        DS["dCw"].issue(g.dma_start(combhbm.ap(), combT[:, :]), 0, "cw")
                        DS["dCw"].wait(g, "cw")
                    else:
                        g.wait_ge(SM["vT1"], 44 * r)
                    DS["dCb"].issue(
                        g.dma_start(comb_bc[:, :],
                                    combhbm.ap()[r:r + 1, :].partition_broadcast(128)[:, :]),
                        0, ("cb", r))
                for hc in range(NHC):
                    c = r * NHC + hc
                    if c >= 2:
                        g.wait_ge(SM["tGa"], 4 * (c - 1))
                        g.wait_ge(SM["tGb"], 4 * (c - 1))
                    DS["dWg"].issue(g.dma_start(gw[c % 2][:, :], gsrc(hc)), c)
                    DS["dWg"].issue(g.dma_start(uw[c % 2][:, :], usrc(hc)), c, ("gu", c))
                    if r == 0:
                        if hc == 0:
                            g.wait_ge(SM["sXn"], NT)
                            g.wait_ge(SM["vX"], NT)
                            g.wait_ge(SM["tTr"], 4 * NT)
                    else:
                        g.wait_ge(SM["tDn"], 64 * r)
                    DS["dWd"].issue(
                        g.dma_start(wdb[:, hc * 2048:(hc + 1) * 2048], dsrc(hc)),
                        0, ("wd", r) if hc == NHC - 1 else None)

        @blk.sync
        def _(sy):
            DS["dB"].issue(sy.dma_start(b_dkv[:, :], dkvb.ap()), 0)
            DS["dB"].issue(sy.dma_start(b_dq[:, :], dqb.ap()), 0)
            DS["dB"].issue(sy.dma_start(b_uk[:, :], ukb.ap()), 0)
            DS["dB"].issue(sy.dma_start(b_uq[:, :], uqb.ap()), 0)
            DS["dB"].issue(sy.dma_start(b_wo[:, :], wob.ap()), 0, "bias")
            for tt in range(NT):
                if tt >= 2:
                    sy.wait_ge(SM["vX"], tt - 1)
                    sy.wait_ge(SM["sXn"], tt - 1)
                DS["dX"].issue(sy.dma_start(xt[tt % 2][:, :],
                                            xin.ap()[tt * 128:(tt + 1) * 128, :]),
                               tt, ("x", tt))
            NU = NR * 64
            for u in range(NU):
                rd = u + 3
                if 64 <= rd < NU:
                    dc, tg = (rd % 64) // 4, rd % 4
                    DS["dRw"].wait(sy, ("rw", rd - 64))
                    sy.wait_ge(SM["vDn"], rd - 2)
                    DS["dRr"].issue(
                        sy.dma_start(rmwb[rd % 3][:, :],
                                     partial.ap()[dc * 128:(dc + 1) * 128,
                                                  tg * 512:(tg + 1) * 512]),
                        rd, ("rr", rd))
                dc, tg = (u % 64) // 4, u % 4
                sy.wait_ge(SM["vDn"], u + 1)
                DS["dRw"].issue(
                    sy.dma_start(partial.ap()[dc * 128:(dc + 1) * 128,
                                              tg * 512:(tg + 1) * 512],
                                 outb[u % 4][:, :]),
                    u, ("rw", u))
            for uu in range(NU - 4, NU):
                DS["dRw"].wait(sy, ("rw", uu))

        @blk.vector
        def _(v):
            v.wait_ge(SM["gI"], 3)
            v.tensor_single_scalar(ident[:, :], io1[:, :], ioc[:, :], ALU.is_equal
                                   ).then_inc(SM["vI"], 1)
            for tt in range(NT):
                p = tt % 2
                DS["dX"].wait(v, ("x", tt))
                if tt >= 2:
                    v.wait_ge(SM["sXn"], tt - 1)
                v.reduce_sum(sums[p][:, :], xt[p][:, :], axis=AX.X).then_inc(SM["vX"], 1)
                v.wait_ge(SM["vX"], tt + 1)
                v.wait_ge(SM["sSq"], tt + 1)
                v.wait_ge(SM["vW"], vw_count[0])
                vi(v.tensor_scalar_mul(stat[p][:, 0:1], sums[p][:, :], 1.0 / D))
                vi(v.tensor_scalar_mul(stat[p][:, 1:2], ssq[p][:, :], 1.0 / D))
                v.wait_ge(SM["vW"], vw_count[0])
                vi(v.tensor_mul(stat[p][:, 2:3], stat[p][:, 0:1], stat[p][:, 0:1]))
                v.wait_ge(SM["vW"], vw_count[0])
                v.tensor_scalar(stat[p][:, 4:5], stat[p][:, 1:2], stat[p][:, 2:3],
                                EPS, ALU.subtract, ALU.add).then_inc(SM["vVar"], 1)
                v.wait_ge(SM["sSd"], tt + 1)
                vi(v.reciprocal(stat[p][:, 5:6], sd[p][:, :]))
                v.wait_ge(SM["vW"], vw_count[0])
                v.scalar_tensor_tensor(stat[p][:, 6:7], stat[p][:, 0:1], -1.0,
                                       stat[p][:, 5:6], ALU.mult, ALU.mult
                                       ).then_inc(SM["vSt"], 1)
                for gidx in range(4):
                    G = 4 * tt + gidx
                    v.wait_ge(SM["tTr"], G + 1)
                    src = trp_of(G).rearrange("p (a b) -> p a b", a=4)
                    dst = xnT3[:, 4 * gidx:4 * gidx + 4, tt * 128:(tt + 1) * 128]
                    v.tensor_copy(dst, src).then_inc(SM["vTr"], 1)
            for tt in range(NT):
                p = tt % 2
                if tt == 0:
                    v.wait_ge(SM["tLg"], NT)
                v.reduce_max(negA[:, tt:tt + 1], lgp_of(tt), axis=AX.X, negate=True
                             ).then_inc(SM["vLa"], 1)
                v.wait_ge(SM["sLx"], tt + 1)
                vi(v.reciprocal(inv[:, :], sume[p][:, :]))
                v.wait_ge(SM["vW"], vw_count[0])
                vi(v.tensor_scalar_mul(prb[p][:, :], prb[p][:, :], inv[:, :]))
                v.wait_ge(SM["vW"], vw_count[0])
                vi(v.reduce_max(negb[:, :], prb[p][:, :], axis=AX.X, negate=True))
                v.wait_ge(SM["vW"], vw_count[0])
                vi(v.tensor_scalar_mul(negb[:, :], negb[:, :], -1.0))
                v.wait_ge(SM["vW"], vw_count[0])
                vi(v.tensor_single_scalar(lgw[:, :], prb[p][:, :], negb[:, :], ALU.is_ge))
                v.wait_ge(SM["vW"], vw_count[0])
                vi(v.tensor_sub(lgw[:, :], prb[p][:, :], lgw[:, :]))
                v.wait_ge(SM["vW"], vw_count[0])
                vi(v.reduce_max(negb[:, :], lgw[:, :], axis=AX.X, negate=True))
                v.wait_ge(SM["vW"], vw_count[0])
                vi(v.tensor_scalar_mul(negb[:, :], negb[:, :], -1.0))
                v.wait_ge(SM["vW"], vw_count[0])
                v.scalar_tensor_tensor(cmb[p][:, :], prb[p][:, :], negb[:, :],
                                       prb[p][:, :], ALU.is_ge, ALU.mult
                                       ).then_inc(SM["vCb"], 1)
                v.wait_ge(SM["tCT"], tt + 1)
                v.tensor_copy(combT[:, tt * 128:(tt + 1) * 128], ctp_of(p)[0:8, :]
                              ).then_inc(SM["vCT"], 1)
            for r in range(NR):
                if r == E:
                    v.memset(comb_bc[:, :], 1.0).then_inc(SM["vCbc"], 1)
                for hc in range(NHC):
                    for tg in range(4):
                        G = (r * NHC + hc) * 4 + tg
                        v.wait_ge(SM["tGb"], G + 1)
                        if r < E:
                            DS["dCb"].wait(v, ("cb", r))
                        else:
                            v.wait_ge(SM["vCbc"], 1)
                        v.tensor_tensor(upb[:, tg * 512:(tg + 1) * 512], ups[G % 2],
                                        comb_bc[:, tg * 512:(tg + 1) * 512], ALU.mult
                                        ).then_inc(SM["vT1"], 1)
                        v.wait_ge(SM["sSl"], G + 1)
                        v.wait_ge(SM["vT1"], G + 1)
                        v.tensor_tensor(
                            gpr[:, hc * 2048 + tg * 512: hc * 2048 + (tg + 1) * 512],
                            sgb[:, tg * 512:(tg + 1) * 512],
                            upb[:, tg * 512:(tg + 1) * 512], ALU.mult
                        ).then_inc(SM["vT2"], 1)
                for dc in range(ND):
                    for tg in range(4):
                        u = r * 64 + dc * 4 + tg
                        v.wait_ge(SM["tDn"], u + 1)
                        if u >= 4:
                            DS["dRw"].wait(v, ("rw", u - 4))
                        if r == 0:
                            v.tensor_copy(outb[u % 4][:, :], dps[u % 2]).then_inc(SM["vDn"], 1)
                        else:
                            DS["dRr"].wait(v, ("rr", u))
                            v.tensor_tensor(outb[u % 4][:, :], dps[u % 2], rmwb[u % 3][:, :],
                                            ALU.add).then_inc(SM["vDn"], 1)

        @blk.scalar
        def _(s):
            def sq(tt):
                p = tt % 2
                DS["dX"].wait(s, ("x", tt))
                if tt >= 2:
                    s.wait_ge(SM["tTr"], 4 * (tt - 1))
                    s.wait_ge(SM["vSt"], tt - 1)
                s.activation(xnm[p][:, :], xt[p][:, :], AF.Square,
                             accum_out=ssq[p][:, :]).then_inc(SM["sSq"], 1)

            sq(0)
            for tt in range(NT):
                p = tt % 2
                if tt + 1 < NT:
                    sq(tt + 1)
                s.wait_ge(SM["vVar"], tt + 1)
                s.activation(sd[p][:, :], stat[p][:, 4:5], AF.Sqrt).then_inc(SM["sSd"], 1)
                s.wait_ge(SM["vSt"], tt + 1)
                s.activation(xnm[p][:, :], xt[p][:, :], AF.Identity,
                             bias=stat[p][:, 6:7], scale=stat[p][:, 5:6]
                             ).then_inc(SM["sXn"], 1)
            for tt in range(NT):
                p = tt % 2
                s.wait_ge(SM["vLa"], tt + 1)
                if tt >= 2:
                    s.wait_ge(SM["vCb"], tt - 1)
                s.activation(prb[p][:, :], lgp_of(tt), AF.Exp, bias=negA[:, tt:tt + 1],
                             accum_out=sume[p][:, :]).then_inc(SM["sLx"], 1)
            for r in range(NR):
                for hc in range(NHC):
                    for tg in range(4):
                        G = (r * NHC + hc) * 4 + tg
                        s.wait_ge(SM["tGa"], G + 1)
                        if G >= 4:
                            s.wait_ge(SM["vT2"], G - 3)
                        s.activation(sgb[:, tg * 512:(tg + 1) * 512], gps[G % 2], AF.Silu
                                     ).then_inc(SM["sSl"], 1)

        @blk.tensor
        def _(t):
            t.wait_ge(SM["vI"], 1)
            for tt in range(NT):
                p = tt % 2
                for gidx in range(4):
                    G = 4 * tt + gidx
                    if gidx == 0:
                        t.wait_ge(SM["sXn"], tt + 1)
                    if G >= 2:
                        t.wait_ge(SM["vTr"], G - 1)
                    trp = trp_of(G)
                    for j in range(4):
                        inst = t.transpose(
                            trp[:, j * 128:(j + 1) * 128],
                            xnm[tt % 2][:, (4 * gidx + j) * 128:(4 * gidx + j + 1) * 128],
                            ident[:, :])
                    inst.then_inc(SM["tTr"], 1)
                if tt == 0:
                    DS["dG"].wait(t, "setup")
                t.wait_ge(SM["vTr"], 4 * (tt + 1))
                for dc in range(ND):
                    inst = t.matmul(lgp_of(tt), xnT3[:, dc, tt * 128:(tt + 1) * 128],
                                    gwv[:, dc * 8:(dc + 1) * 8],
                                    start=(dc == 0), stop=(dc == ND - 1))
                inst.then_inc(SM["tLg"], 1)
            for tt in range(NT):
                p = tt % 2
                t.wait_ge(SM["vCb"], tt + 1)
                if tt >= 2:
                    t.wait_ge(SM["vCT"], tt - 1)
                t.transpose(ctp_of(p)[0:8, :], cmb[p][:, :], ident[:, :]).then_inc(SM["tCT"], 1)
            for r in range(NR):
                for hc in range(NHC):
                    c = r * NHC + hc
                    for tg in range(4):
                        G = c * 4 + tg
                        if tg == 0:
                            DS["dWg"].wait(t, ("gu", c))
                        if G >= 2:
                            t.wait_ge(SM["sSl"], G - 1)
                        for dc in range(ND):
                            inst = t.matmul(gps[G % 2], gw[c % 2][:, dc * 128:(dc + 1) * 128],
                                            xnT3[:, dc, tg * 512:(tg + 1) * 512],
                                            start=(dc == 0), stop=(dc == ND - 1))
                        inst.then_inc(SM["tGa"], 1)
                        if G >= 2:
                            t.wait_ge(SM["vT1"], G - 1)
                        for dc in range(ND):
                            inst = t.matmul(ups[G % 2], uw[c % 2][:, dc * 128:(dc + 1) * 128],
                                            xnT3[:, dc, tg * 512:(tg + 1) * 512],
                                            start=(dc == 0), stop=(dc == ND - 1))
                        inst.then_inc(SM["tGb"], 1)
                for dc in range(ND):
                    for tg in range(4):
                        u = r * 64 + dc * 4 + tg
                        if dc == 0 and tg == 0:
                            DS["dWd"].wait(t, ("wd", r))
                            t.wait_ge(SM["vT2"], 44 * (r + 1))
                        if u >= 2:
                            t.wait_ge(SM["vDn"], u - 1)
                        for hc in range(NHC):
                            inst = t.matmul(
                                dps[u % 2],
                                wdb[:, hc * 2048 + dc * 128: hc * 2048 + (dc + 1) * 128],
                                gpr[:, hc * 2048 + tg * 512: hc * 2048 + (tg + 1) * 512],
                                start=(hc == 0), stop=(hc == NHC - 1))
                        inst.then_inc(SM["tDn"], 1)

    st.close()

    st = ExitStack()
    ckv = st.enter_context(nc.sbuf_tensor("ckv", [128, NCC * S], BF16, side="right"))
    cq = st.enter_context(nc.sbuf_tensor("cq", [128, NQC * S], BF16, side="right"))
    st3 = ExitStack()
    dkvw = [st3.enter_context(nc.sbuf_tensor(f"dkvw{p}", [128, 2048], BF16, side="right"))
            for p in range(2)]
    dqw = [st3.enter_context(nc.sbuf_tensor(f"dqw{p}", [128, 2048], BF16, side="right"))
           for p in range(2)]

    ckv3 = ckv.ap().rearrange("p (cc t) -> p cc t", cc=NCC)
    cq3 = cq.ap().rearrange("p (qc t) -> p qc t", qc=NQC)
    pjp = [B[:, 1024:1536], B[:, 1536:2048]]
    wotA = nc.alloc_sbuf_tensor_at("wotA", [128, 12 * 2048], BF16,
                                   offset=nc.lookup_mloc(cq).addr)
    wotB = nc.alloc_sbuf_tensor_at("wotB", [128, 4 * 2048], BF16,
                                   offset=nc.lookup_mloc(ckv).addr)

    with nc.Block() as blk:
        @blk.gpsimd
        def _(g):
            for cc in range(NCC):
                if cc >= 2:
                    g.wait_ge(SM["tPj"], 4 * (cc - 1))
                DS["dPw"].issue(g.dma_start(dkvw[cc % 2][:, :], dkvT.ap()[cc]),
                                cc, ("dkv", cc))
            for qc in range(NQC):
                if qc >= 2:
                    g.wait_ge(SM["tPj"], 16 + 4 * (qc - 1))
                DS["dQw"].issue(g.dma_start(dqw[qc % 2][:, :], dqT.ap()[qc]),
                                qc, ("dq", qc))

        @blk.tensor
        def _(t):
            for cc in range(NCC):
                for tg in range(4):
                    j = cc * 4 + tg
                    if tg == 0:
                        DS["dPw"].wait(t, ("dkv", cc))
                    if j >= 2:
                        t.wait_ge(SM["sPj"], j - 1)
                    for dc in range(ND):
                        inst = t.matmul(pjp[j % 2], dkvw[cc % 2][:, dc * 128:(dc + 1) * 128],
                                        xnT3[:, dc, tg * 512:(tg + 1) * 512],
                                        start=(dc == 0), stop=(dc == ND - 1))
                    inst.then_inc(SM["tPj"], 1)
            for qc in range(NQC):
                for tg in range(4):
                    j = 16 + qc * 4 + tg
                    if tg == 0:
                        DS["dQw"].wait(t, ("dq", qc))
                    t.wait_ge(SM["sPj"], j - 1)
                    for dc in range(ND):
                        inst = t.matmul(pjp[j % 2], dqw[qc % 2][:, dc * 128:(dc + 1) * 128],
                                        xnT3[:, dc, tg * 512:(tg + 1) * 512],
                                        start=(dc == 0), stop=(dc == ND - 1))
                    inst.then_inc(SM["tPj"], 1)

        @blk.scalar
        def _(s):
            DS["dB"].wait(s, "bias")
            for cc in range(NCC):
                for tg in range(4):
                    j = cc * 4 + tg
                    s.wait_ge(SM["tPj"], j + 1)
                    s.activation(ckv3[:, cc, tg * 512:(tg + 1) * 512], pjp[j % 2], AF.Identity,
                                 bias=b_dkv[:, cc:cc + 1]).then_inc(SM["sPj"], 1)
            for qc in range(NQC):
                for tg in range(4):
                    j = 16 + qc * 4 + tg
                    s.wait_ge(SM["tPj"], j + 1)
                    s.activation(cq3[:, qc, tg * 512:(tg + 1) * 512], pjp[j % 2], AF.Identity,
                                 bias=b_dq[:, qc:qc + 1]).then_inc(SM["sPj"], 1)

    st3.close()

    kh = st.enter_context(nc.sbuf_tensor("kh", [128, S], BF16, side="right"))
    qh = st.enter_context(nc.sbuf_tensor("qh", [128, S], BF16, side="right"))
    vh = [st.enter_context(nc.sbuf_tensor(f"vh{p}", [128, NT * 132], BF16, side="right"))
          for p in range(2)]
    atm = st.enter_context(nc.sbuf_tensor("atm", [128, S], BF16, side="right"))
    ukw = [st.enter_context(nc.sbuf_tensor(f"ukw{p}", [128, 512], BF16, side="right"))
           for p in range(2)]
    uvw = [st.enter_context(nc.sbuf_tensor(f"uvw{p}", [128, 512], BF16, side="right"))
           for p in range(2)]
    uqw = [st.enter_context(nc.sbuf_tensor(f"uqw{p}", [128, 1536], BF16, side="right"))
           for p in range(2)]
    pt = [st.enter_context(nc.sbuf_tensor(f"pt{p}", [128, 16 * 512], BF16, side="right"))
          for p in range(2)]

    vh3 = [vv.ap().rearrange("p (kt c) -> p kt c", kt=NT) for vv in vh]
    avp = [B[:, 0:512], B[:, 512:1024], B[:, 1024:1536], B[:, 1536:2048]]
    attnT = xnT
    attnT3 = xnT3
    pj4 = [A[:, 1024:1536], A[:, 1536:2048]]

    def trp4_of(gt):
        return (A[:, 0:256] if gt % 2 == 0 else A[:, 512:768]).bitcast(BF16)

    def proj_consumer(ju):
        h, idx = ju // 24, ju % 24
        if idx < 4:
            return ("sKQ", 8 * h + idx + 1)
        if idx < 8:
            return ("sKQ", 8 * h + idx + 1)
        return ("vV", 16 * h + (idx - 8) + 1)

    with nc.Block() as blk:
        @blk.gpsimd
        def _(g):
            g.memset(vh3[0][:, :, 128:129], 1.0).then_inc(SM["gOn"], 1)
            g.memset(vh3[1][:, :, 128:129], 1.0).then_inc(SM["gOn"], 1)
            for h in range(H):
                if h == 0:
                    g.wait_ge(SM["tPj"], 64)
                if h >= 2:
                    g.wait_ge(SM["tKQ"], 24 * (h - 1))
                DS["dHw"].issue(g.dma_start(ukw[h % 2][:, :], ukT.ap()[h]), h)
                DS["dHw"].issue(g.dma_start(uvw[h % 2][:, :], uvT.ap()[h]), h)
                DS["dHw"].issue(g.dma_start(uqw[h % 2][:, :], uqT.ap()[h]), h, ("h", h))
            g.wait_ge(SM["tKQ"], 24 * H)
            for hc in range(ND):
                dst = wotA[:, hc * 2048:(hc + 1) * 2048] if hc < 12 else \
                    wotB[:, (hc - 12) * 2048:(hc - 12 + 1) * 2048]
                DS["dWo"].issue(g.dma_start(dst, woT.ap()[hc]), hc, ("wo", hc))

        @blk.tensor
        def _(t):
            for h in range(H):
                DS["dHw"].wait(t, ("h", h))
                if h == 0:
                    t.wait_ge(SM["gOn"], 2)
                    t.wait_ge(SM["sPj"], 64)
                for idx in range(24):
                    ju = 24 * h + idx
                    if idx == 0 and h >= 1:
                        t.wait_ge(SM["sEx"], 20 * h)
                    if ju >= 2:
                        sem, val = proj_consumer(ju - 2)
                        t.wait_ge(SM[sem], val)
                    pp = pj4[ju % 2]
                    if idx < 4:
                        tg = idx
                        for cc in range(NCC):
                            inst = t.matmul(pp, ukw[h % 2][:, cc * 128:(cc + 1) * 128],
                                            ckv3[:, cc, tg * 512:(tg + 1) * 512],
                                            start=(cc == 0), stop=(cc == NCC - 1))
                    elif idx < 8:
                        tg = idx - 4
                        for qc in range(NQC):
                            inst = t.matmul(pp, uqw[h % 2][:, qc * 128:(qc + 1) * 128],
                                            cq3[:, qc, tg * 512:(tg + 1) * 512],
                                            start=(qc == 0), stop=(qc == NQC - 1))
                    else:
                        kt = idx - 8
                        for cc in range(NCC):
                            inst = t.matmul(pp[:, 0:128],
                                            ckv3[:, cc, kt * 128:(kt + 1) * 128],
                                            uvw[h % 2][:, cc * 128:(cc + 1) * 128],
                                            start=(cc == 0), stop=(cc == NCC - 1))
                    inst.then_inc(SM["tKQ"], 1)
                def av_group(gq, qsub):
                    hh, qgg = gq // 4, gq % 4
                    q16 = gq * 4 + qsub
                    if qsub == 0:
                        t.wait_ge(SM["sEx"], 5 * gq + 5)
                        if qgg == 0:
                            t.wait_ge(SM["vV"], 16 * (hh + 1))
                    if q16 >= 4:
                        t.wait_ge(SM["vAv"], 2 * (q16 - 3))
                    for kt in range(NT):
                        inst = t.matmul(
                            avp[q16 % 4][:, 0:129],
                            pt[gq % 2][:, kt * 512 + qsub * 128:
                                       kt * 512 + (qsub + 1) * 128],
                            vh3[hh % 2][:, kt, 0:129],
                            start=(kt == 0), stop=(kt == NT - 1))
                    inst.then_inc(SM["tAV"], 1)

                def transposes_of(hh, sex_done):
                    t.wait_ge(SM["sEx"], sex_done)
                    for gidx in range(4):
                        gt = 4 * hh + gidx
                        t.wait_ge(SM["vAv"], 2 * (hh * 16 + 4 * (gidx + 1)))
                        if gt >= 2:
                            t.wait_ge(SM["vAT"], gt - 1)
                        trp = trp4_of(gt)
                        for j in range(4):
                            inst = t.transpose(
                                trp[:, j * 128:(j + 1) * 128],
                                atm[:, (4 * gidx + j) * 128:(4 * gidx + j + 1) * 128],
                                ident[:, :])
                        inst.then_inc(SM["tAT"], 1)

                for qg in range(4):
                    gq = h * 4 + qg
                    base16 = gq * 16
                    for kt in range(NT):
                        if kt == 0 and qg == 0:
                            t.wait_ge(SM["sKQ"], 8 * (h + 1))
                        if kt == 0 and qg == 1 and h >= 1:
                            t.wait_ge(SM["vAT"], 4 * h)
                        if qg == 0 and kt == 2:
                            t.wait_ge(SM["vV"], 16 * h + 15)
                        if qg == 0 and kt == 3:
                            t.wait_ge(SM["vV"], 16 * h + 16)
                        if base16 + kt >= 4:
                            gl = base16 + kt - 4
                            gqp, rr = divmod(gl, 16)
                            mm_ = (rr // 4 + 1) if rr < 12 else (4 if rr < 14 else 5)
                            t.wait_ge(SM["sEx"], 5 * gqp + mm_)
                        t.matmul(A[:, (kt % 4) * 512:(kt % 4 + 1) * 512],
                                 kh[:, kt * 128:(kt + 1) * 128],
                                 qh[:, qg * 512:(qg + 1) * 512],
                                 start=True, stop=True).then_inc(SM["tS"], 1)
                        if kt % 4 == 3 and gq >= 1:
                            av_group(gq - 1, kt // 4)
                            if gq % 4 == 0 and kt == 15:
                                transposes_of(h - 1, 5 * (4 * h) + 5)
                if h == H - 1:
                    for qsub in range(4):
                        av_group(4 * H - 1, qsub)
                    transposes_of(H - 1, 20 * H)

        @blk.scalar
        def _(s):
            DS["dB"].wait(s, "bias")
            for h in range(H):
                for qg in range(4):
                    gq = h * 4 + qg
                    base16 = gq * 16
                    for lo, hi in ((0, 4), (4, 8), (8, 12), (12, 14), (14, 16)):
                        s.wait_ge(SM["tS"], base16 + hi)
                        if lo == 0 and gq >= 2:
                            s.wait_ge(SM["tAV"], (gq - 1) * 4)
                        b0, b1 = (lo % 4) * 512, (lo % 4) * 512 + (hi - lo) * 512
                        s.activation(pt[gq % 2][:, lo * 512:hi * 512],
                                     A[:, b0:b1],
                                     AF.Exp, scale=SCALE).then_inc(SM["sEx"], 1)

        @blk.vector
        def _(v):
            DS["dG"].wait(v, "setup")

            def attn_evacs(hh, qlo, qhi):
                for qs in range(qlo, qhi):
                    q16 = hh * 16 + qs
                    v.wait_ge(SM["tAV"], q16 + 1)
                    if q16 >= 1:
                        v.wait_ge(SM["vAv"], 2 * q16)
                    vi(v.reciprocal(inv[:, :], avp[q16 % 4][:, 128:129]))
                    v.wait_ge(SM["vW"], vw_count[0])
                    v.tensor_scalar_mul(atm[:, qs * 128:(qs + 1) * 128],
                                        avp[q16 % 4][:, 0:128], inv[:, :]
                                        ).then_inc(SM["vAv"], 2)

            def attnT_evacs(hh):
                for gidx in range(4):
                    gt = 4 * hh + gidx
                    v.wait_ge(SM["tAT"], gt + 1)
                    v.tensor_copy(attnT3[:, hh, gidx * 512:(gidx + 1) * 512], trp4_of(gt)
                                  ).then_inc(SM["vAT"], 1)

            for h in range(H):
                for idx in range(8):
                    ju = 24 * h + idx
                    v.wait_ge(SM["tKQ"], ju + 1)
                    if idx < 4:
                        v.tensor_scalar_add(kh[:, idx * 512:(idx + 1) * 512], pj4[ju % 2],
                                            b_uk[:, h:h + 1]).then_inc(SM["sKQ"], 1)
                    else:
                        tg = idx - 4
                        v.tensor_scalar_add(qh[:, tg * 512:(tg + 1) * 512], pj4[ju % 2],
                                            b_uq[:, h:h + 1]).then_inc(SM["sKQ"], 1)
                for kt in range(NT):
                    ju = 24 * h + 8 + kt
                    if kt == 0 and h >= 2:
                        v.wait_ge(SM["tAV"], 16 * (h - 1))
                    v.wait_ge(SM["tKQ"], ju + 1)
                    v.tensor_tensor(vh3[h % 2][:, kt, 0:128], pj4[ju % 2][:, 0:128],
                                    uvb_bc[:, h * 128:(h + 1) * 128], ALU.add
                                    ).then_inc(SM["vV"], 1)
                if h >= 1:
                    attn_evacs(h - 1, 12, 16)
                    attnT_evacs(h - 1)
                attn_evacs(h, 0, 12)
            attn_evacs(H - 1, 12, 16)
            attnT_evacs(H - 1)

    st.close()

    st = ExitStack()
    prt = [st.enter_context(nc.sbuf_tensor(f"prt{p}", [128, 512], F32, side="left"))
           for p in range(2)]
    xtb = [st.enter_context(nc.sbuf_tensor(f"xtb{p}", [128, 512], F32, side="left"))
           for p in range(2)]
    tmpb = [st.enter_context(nc.sbuf_tensor(f"tmpb{p}", [128, 512], F32, side="left"))
            for p in range(2)]
    fout = [st.enter_context(nc.sbuf_tensor(f"fout{p}", [128, 512], F32, side="left"))
            for p in range(2)]

    def wot_tile(hc, oc):
        if hc < 12:
            return wotA[:, hc * 2048 + oc * 128: hc * 2048 + (oc + 1) * 128]
        return wotB[:, (hc - 12) * 2048 + oc * 128: (hc - 12) * 2048 + (oc + 1) * 128]

    wops = [A[:, 0:512], A[:, 512:1024], A[:, 1024:1536], A[:, 1536:2048]]

    with nc.Block() as blk:
        @blk.sync
        def _(sy):
            for uu in range(NR * 64 - 4, NR * 64):
                DS["dRw"].wait(sy, ("rw", uu))
            for j in range(-2, 64):
                rd = j + 2
                if rd < 64:
                    oc, tg = rd // 4, rd % 4
                    if rd >= 2:
                        sy.wait_ge(SM["vWo"], 2 * (rd - 2) + 2)
                    DS["dPr"].issue(
                        sy.dma_start(prt[rd % 2][:, :],
                                     partial.ap()[oc * 128:(oc + 1) * 128,
                                                  tg * 512:(tg + 1) * 512]),
                        rd, ("pr", rd))
                    DS["dXr"].issue(
                        sy.dma_start(xtb[rd % 2][:, :],
                                     xTd.ap()[oc * 128:(oc + 1) * 128,
                                              tg * 512:(tg + 1) * 512]),
                        rd, ("xr", rd))
                if j >= 0:
                    oc, tg = j // 4, j % 4
                    sy.wait_ge(SM["vWo"], 2 * j + 2)
                    DS["dOw"].issue(
                        sy.dma_start(outT.ap()[oc * 128:(oc + 1) * 128,
                                               tg * 512:(tg + 1) * 512],
                                     fout[j % 2][:, :]),
                        j, ("ow", j))
            DS["dOw"].wait(sy, ("ow", 62))
            DS["dOw"].wait(sy, ("ow", 63))

        @blk.tensor
        def _(t):
            t.wait_ge(SM["vAT"], 4 * H)
            for j in range(64):
                oc, tg = j // 4, j % 4
                if j >= 4:
                    t.wait_ge(SM["vWo"], 2 * (j - 4) + 2)
                for hc in range(ND):
                    if j == 0:
                        DS["dWo"].wait(t, ("wo", hc))
                    elif j == 1 and hc == 0:
                        DS["dWo"].wait(t, ("wo", ND - 1))
                    inst = t.matmul(wops[j % 4], wot_tile(hc, oc),
                                    attnT3[:, hc, tg * 512:(tg + 1) * 512],
                                    start=(hc == 0), stop=(hc == ND - 1))
                inst.then_inc(SM["tWo"], 1)

        @blk.vector
        def _(v):
            DS["dB"].wait(v, "bias")
            for j in range(64):
                oc = j // 4
                v.wait_ge(SM["tWo"], j + 1)
                DS["dPr"].wait(v, ("pr", j))
                DS["dXr"].wait(v, ("xr", j))
                if j >= 2:
                    v.wait_ge(SM["vWo"], 2 * (j - 1))
                vi(v.scalar_tensor_tensor(tmpb[j % 2][:, :], wops[j % 4], b_wo[:, oc:oc + 1],
                                          prt[j % 2][:, :], ALU.add, ALU.add))
                if j >= 2:
                    DS["dOw"].wait(v, ("ow", j - 2))
                v.wait_ge(SM["vW"], vw_count[0])
                v.tensor_add(fout[j % 2][:, :], tmpb[j % 2][:, :], xtb[j % 2][:, :]
                             ).then_inc(SM["vWo"], 2)

    st.close()
    return nc



def _prep_weights(inputs):
    f32 = np.float32

    def C(a):
        return np.ascontiguousarray(a, dtype=f32)

    W = {}
    W["dkvT"] = C(inputs["L0_attn_W_DKV_w"].reshape(NCC, 128, ND, 128).transpose(0, 3, 2, 1)
                  .reshape(NCC, 128, 2048))
    W["dqT"] = C(inputs["L0_attn_W_DQ_w"].reshape(NQC, 128, ND, 128).transpose(0, 3, 2, 1)
                 .reshape(NQC, 128, 2048))
    W["ukT"] = C(inputs["L0_attn_W_UK_w"].reshape(H, 128, NCC, 128).transpose(0, 3, 2, 1)
                 .reshape(H, 128, 512))
    W["uvT"] = C(inputs["L0_attn_W_UV_w"].reshape(H, 128, NCC, 128).transpose(0, 3, 2, 1)
                 .reshape(H, 128, 512))
    W["uqT"] = C(inputs["L0_attn_W_UQ_w"].reshape(H, 128, NQC, 128).transpose(0, 3, 2, 1)
                 .reshape(H, 128, 1536))
    W["woT"] = C(inputs["L0_attn_W_O_w"].reshape(ND, 128, ND, 128).transpose(2, 3, 0, 1)
                 .reshape(ND, 128, 2048))
    W["gwp"] = C(inputs["L0_ffn_gate_w"].reshape(ND, 128, 8).transpose(1, 0, 2)
                 .reshape(128, 128))
    W["wgp"] = C(inputs["L0_ffn_w_gate"].reshape(E, ND, 128, NHC, 128)
                 .transpose(0, 3, 2, 1, 4).reshape(E, NHC, 128, 2048))
    W["wup"] = C(inputs["L0_ffn_w_up"].reshape(E, ND, 128, NHC, 128)
                 .transpose(0, 3, 2, 1, 4).reshape(E, NHC, 128, 2048))
    W["wdp"] = C(inputs["L0_ffn_w_down"].reshape(E, NHC, 128, 2048))
    W["swgp"] = C(inputs["L0_ffn_sw_gate"].reshape(ND, 128, 22, 128).transpose(2, 1, 0, 3)
                  .reshape(22, 128, 2048))
    W["swup"] = C(inputs["L0_ffn_sw_up"].reshape(ND, 128, 22, 128).transpose(2, 1, 0, 3)
                  .reshape(22, 128, 2048))
    W["swdp"] = C(inputs["L0_ffn_sw_down"].reshape(22, 128, 2048))
    W["dkvb"] = C(inputs["L0_attn_W_DKV_b"].reshape(NCC, 128).T)
    W["dqb"] = C(inputs["L0_attn_W_DQ_b"].reshape(NQC, 128).T)
    W["ukb"] = C(inputs["L0_attn_W_UK_b"].reshape(H, 128).T)
    W["uqb"] = C(inputs["L0_attn_W_UQ_b"].reshape(H, 128).T)
    W["wob"] = C(inputs["L0_attn_W_O_b"].reshape(ND, 128).T)
    W["uvbr"] = C(inputs["L0_attn_W_UV_b"].reshape(1, 2048))
    return W


def kernel(**inputs):
    x = np.asarray(inputs["L0_attn_hidden_states"], dtype=np.float32)
    W = _prep_weights(inputs)
    nc = build_nc()
    in_maps = []
    for c in range(NCORES):
        m = dict(W)
        m["xin"] = np.ascontiguousarray(x[c])
        m["xTd"] = np.ascontiguousarray(x[c].T)
        in_maps.append(m)
    res = run_bass_kernel_spmd(nc, in_maps, core_ids=list(range(NCORES)))
    out = np.stack([np.asarray(res.results[c]["outT"], dtype=np.float32).T
                    for c in range(NCORES)], axis=0)
    return np.ascontiguousarray(out)


# version 14
# speedup vs baseline: 1.0893x; 1.0893x over previous
import math
from contextlib import ExitStack

import numpy as np

import concourse.bass as bass
import concourse.mybir as mybir
from concourse.bass_utils import run_bass_kernel_spmd

F32 = mybir.dt.float32
BF16 = mybir.dt.bfloat16
I32 = mybir.dt.int32
AF = mybir.ActivationFunctionType
ALU = mybir.AluOpType
AX = mybir.AxisListType

NCORES = 8
S = 2048
D = 2048
NT = S // 128
ND = D // 128
H = 16
KVC, QC = 512, 1536
NCC, NQC = KVC // 128, QC // 128
E, HF = 8, 1408
NHC = HF // 128
NR = E + 2
NTG = S // 512
SCALE = 1.0 / math.sqrt(128.0)
EPS = 1e-5


def build_nc():
    nc = bass.Bass("TRN2", target_bir_lowering=False)

    xin = nc.dram_tensor("xin", [S, D], F32, kind="ExternalInput")
    xTd = nc.dram_tensor("xTd", [D, S], F32, kind="ExternalInput")
    dkvT = nc.dram_tensor("dkvT", [NCC, 128, 2048], F32, kind="ExternalInput")
    dqT = nc.dram_tensor("dqT", [NQC, 128, 2048], F32, kind="ExternalInput")
    ukT = nc.dram_tensor("ukT", [H, 128, 512], F32, kind="ExternalInput")
    uvT = nc.dram_tensor("uvT", [H, 128, 512], F32, kind="ExternalInput")
    uqT = nc.dram_tensor("uqT", [H, 128, 1536], F32, kind="ExternalInput")
    woT = nc.dram_tensor("woT", [ND, 128, 2048], F32, kind="ExternalInput")
    gwp = nc.dram_tensor("gwp", [128, 128], F32, kind="ExternalInput")
    wgp = nc.dram_tensor("wgp", [E, NHC, 128, 2048], F32, kind="ExternalInput")
    wup = nc.dram_tensor("wup", [E, NHC, 128, 2048], F32, kind="ExternalInput")
    wdp = nc.dram_tensor("wdp", [E, NHC, 128, 2048], F32, kind="ExternalInput")
    swgp = nc.dram_tensor("swgp", [22, 128, 2048], F32, kind="ExternalInput")
    swup = nc.dram_tensor("swup", [22, 128, 2048], F32, kind="ExternalInput")
    swdp = nc.dram_tensor("swdp", [22, 128, 2048], F32, kind="ExternalInput")
    dkvb = nc.dram_tensor("dkvb", [128, NCC], F32, kind="ExternalInput")
    dqb = nc.dram_tensor("dqb", [128, NQC], F32, kind="ExternalInput")
    ukb = nc.dram_tensor("ukb", [128, H], F32, kind="ExternalInput")
    uqb = nc.dram_tensor("uqb", [128, H], F32, kind="ExternalInput")
    wob = nc.dram_tensor("wob", [128, ND], F32, kind="ExternalInput")
    uvbr = nc.dram_tensor("uvbr", [1, 2048], F32, kind="ExternalInput")

    outT = nc.dram_tensor("outT", [D, S], F32, kind="ExternalOutput")
    partial = nc.dram_tensor("partial", [D, S], F32, kind="Internal")
    combhbm = nc.dram_tensor("combhbm", [8, 2048], BF16, kind="Internal")

    ident = nc.alloc_sbuf_tensor("ident", [128, 128], BF16)
    onesc = nc.alloc_sbuf_tensor("onesc", [1, 128], BF16)
    gwv = nc.alloc_sbuf_tensor("gwv", [128, 128], BF16)
    uvb_bc = nc.alloc_sbuf_tensor("uvb_bc", [128, 2048], BF16)
    b_dkv = nc.alloc_sbuf_tensor("b_dkv", [128, NCC], F32)
    b_dq = nc.alloc_sbuf_tensor("b_dq", [128, NQC], F32)
    b_uk = nc.alloc_sbuf_tensor("b_uk", [128, H], F32)
    b_uq = nc.alloc_sbuf_tensor("b_uq", [128, H], F32)
    b_wo = nc.alloc_sbuf_tensor("b_wo", [128, ND], F32)
    stat = [nc.alloc_sbuf_tensor(f"stat{p}", [128, 16], F32) for p in range(2)]
    ssq = [nc.alloc_sbuf_tensor(f"ssq{p}", [128, 1], F32) for p in range(2)]
    sums = [nc.alloc_sbuf_tensor(f"sums{p}", [128, 1], F32) for p in range(2)]
    sd = [nc.alloc_sbuf_tensor(f"sd{p}", [128, 1], F32) for p in range(2)]
    negm = [nc.alloc_sbuf_tensor(f"negm{p}", [128, 1], F32) for p in range(2)]
    sume = [nc.alloc_sbuf_tensor(f"sume{p}", [128, 1], F32) for p in range(2)]
    prb = [nc.alloc_sbuf_tensor(f"prb{p}", [128, 8], F32) for p in range(2)]
    cmb = [nc.alloc_sbuf_tensor(f"cmb{p}", [128, 8], BF16) for p in range(2)]
    lgw = nc.alloc_sbuf_tensor("lgw", [128, 8], F32)
    negb = nc.alloc_sbuf_tensor("negb", [128, 1], F32)
    negA = nc.alloc_sbuf_tensor("negA", [128, NT], F32)
    inv = nc.alloc_sbuf_tensor("inv", [128, 1], F32)

    xnT = nc.alloc_sbuf_tensor("xnT", [128, ND * S], BF16)

    A = nc.alloc_psum_tensor("A", [128, 2048], F32)
    B = nc.alloc_psum_tensor("B", [128, 2048], F32)

    sem_names = [
        "gI", "vI",
        "sSq", "sSd", "sXn", "vX", "vVar", "vSt", "tTr", "vTr",
        "tLg", "vLa", "sLx", "vCb", "tCT", "vCT", "vW",
        "tGa", "tGb", "sSl", "vT1", "vT2",
        "tCb", "vCbc", "tDn", "vDn",
        "tPj", "sPj",
        "gOn", "tKQ", "sKQ", "vV", "tS", "sEx", "tAV", "vAv", "tAT", "vAT",
        "tWo", "vWo",
    ]
    SM = {n: nc.alloc_semaphore(n) for n in sem_names}
    vw_count = [0]

    class DmaStream:

        def __init__(self, name, nslots):
            self.sems = [nc.alloc_semaphore(f"{name}_{i}") for i in range(nslots)]
            self.tot = [0] * nslots
            self.mark = {}

        def issue(self, inst, slot, label=None):
            k = slot % len(self.sems)
            inst.then_inc(self.sems[k], 16)
            self.tot[k] += 16
            if label is not None:
                self.mark[label] = (self.sems[k], self.tot[k])

        def wait(self, eng, label):
            s, v = self.mark[label]
            eng.wait_ge(s, v)

    DS = {
        "dB": DmaStream("dB", 1), "dG": DmaStream("dG", 1),
        "dX": DmaStream("dX", 2),
        "dWg": DmaStream("dWg", 2), "dWd": DmaStream("dWd", 1),
        "dCw": DmaStream("dCw", 1), "dCb": DmaStream("dCb", 1),
        "dRr": DmaStream("dRr", 3), "dRw": DmaStream("dRw", 4),
        "dPw": DmaStream("dPw", 2), "dQw": DmaStream("dQw", 2), "dHw": DmaStream("dHw", 2),
        "dWo": DmaStream("dWo", 16),
        "dPr": DmaStream("dPr", 2), "dXr": DmaStream("dXr", 2),
        "dOw": DmaStream("dOw", 2),
    }

    def vi(inst):
        inst.then_inc(SM["vW"], 1)
        vw_count[0] += 1
        return inst

    xnT3 = xnT.ap().rearrange("p (dc t) -> p dc t", dc=ND)

    st = ExitStack()
    gw = [st.enter_context(nc.sbuf_tensor(f"gw{p}", [128, 2048], BF16, side="right"))
          for p in range(2)]
    uw = [st.enter_context(nc.sbuf_tensor(f"uw{p}", [128, 2048], BF16, side="right"))
          for p in range(2)]
    sgb = st.enter_context(nc.sbuf_tensor("sgb", [128, 2048], BF16, side="right"))
    upb = st.enter_context(nc.sbuf_tensor("upb", [128, 2048], BF16, side="right"))
    comb_bc = st.enter_context(nc.sbuf_tensor("comb_bc", [128, 2048], BF16, side="right"))
    gpr = st.enter_context(nc.sbuf_tensor("gpr", [128, NHC * 2048], BF16, side="right"))
    wdb = st.enter_context(nc.sbuf_tensor("wdb", [128, NHC * 2048], BF16, side="right"))
    rmwb = [st.enter_context(nc.sbuf_tensor(f"rmwb{p}", [128, 512], F32, side="right"))
            for p in range(3)]
    outb = [st.enter_context(nc.sbuf_tensor(f"outb{p}", [128, 512], F32, side="right"))
            for p in range(4)]
    wdb_addr = nc.lookup_mloc(wdb).addr
    combT = nc.alloc_sbuf_tensor_at("combT", [8, S], BF16, offset=wdb_addr + 40960)
    xt = [nc.alloc_sbuf_tensor_at(f"xt{p}", [128, 2048], F32, offset=wdb_addr + p * 8192)
          for p in range(2)]
    xnm = [nc.alloc_sbuf_tensor_at(f"xnm{p}", [128, 2048], BF16,
                                   offset=wdb_addr + 16384 + p * 4096)
           for p in range(2)]
    sgb_addr = nc.lookup_mloc(sgb).addr
    io1 = nc.alloc_sbuf_tensor_at("io1", [128, 128], F32, offset=sgb_addr)
    ioc = nc.alloc_sbuf_tensor_at("ioc", [128, 1], F32, offset=sgb_addr + 512)

    def lgp_of(tt):
        return A[:, 1024 + 8 * tt: 1024 + 8 * (tt + 1)]

    def trp_of(G):
        return (A[:, 0:256] if G % 2 == 0 else A[:, 512:768]).bitcast(BF16)

    def ctp_of(p):
        return (B[:, 0:64] if p == 0 else B[:, 512:576]).bitcast(BF16)

    gps = [A[:, 0:512], A[:, 512:1024]]
    ups = [A[:, 1024:1536], A[:, 1536:2048]]
    dps = [B[:, 0:512], B[:, 512:1024]]

    def wsrc(r):
        if r < E:
            return (lambda hc: wgp.ap()[r, hc], lambda hc: wup.ap()[r, hc],
                    lambda hc: wdp.ap()[r, hc])
        half = r - E
        return (lambda hc: swgp.ap()[half * NHC + hc], lambda hc: swup.ap()[half * NHC + hc],
                lambda hc: swdp.ap()[half * NHC + hc])

    with nc.Block() as blk:
        @blk.gpsimd
        def _(g):
            g.memset(onesc[:, :], 1.0).then_inc(SM["gI"], 1)
            g.iota(io1[:, :], [[1, 128]], base=0, channel_multiplier=0,
                   allow_small_or_imprecise_dtypes=True).then_inc(SM["gI"], 1)
            g.iota(ioc[:, :], [[0, 1]], base=0, channel_multiplier=1,
                   allow_small_or_imprecise_dtypes=True).then_inc(SM["gI"], 1)
            DS["dG"].issue(g.dma_start(gwv[:, :], gwp.ap()), 0)
            DS["dG"].issue(g.dma_start(uvb_bc[:, :],
                                       uvbr.ap().partition_broadcast(128)[:, :]), 0, "setup")
            for r in range(NR):
                gsrc, usrc, dsrc = wsrc(r)
                if r < E:
                    if r == 0:
                        g.wait_ge(SM["vCT"], NT)
                        DS["dCw"].issue(g.dma_start(combhbm.ap(), combT[:, :]), 0, "cw")
                        DS["dCw"].wait(g, "cw")
                    else:
                        g.wait_ge(SM["vT1"], 44 * r)
                    DS["dCb"].issue(
                        g.dma_start(comb_bc[:, :],
                                    combhbm.ap()[r:r + 1, :].partition_broadcast(128)[:, :]),
                        0, ("cb", r))
                for hc in range(NHC):
                    c = r * NHC + hc
                    if c >= 2:
                        g.wait_ge(SM["tGa"], 4 * (c - 1))
                        g.wait_ge(SM["tGb"], 4 * (c - 1))
                    DS["dWg"].issue(g.dma_start(gw[c % 2][:, :], gsrc(hc)), c)
                    DS["dWg"].issue(g.dma_start(uw[c % 2][:, :], usrc(hc)), c, ("gu", c))
                    if r == 0:
                        if hc == 0:
                            g.wait_ge(SM["sXn"], NT)
                            g.wait_ge(SM["vX"], NT)
                            g.wait_ge(SM["tTr"], 4 * NT)
                    else:
                        g.wait_ge(SM["tDn"], 64 * r)
                    DS["dWd"].issue(
                        g.dma_start(wdb[:, hc * 2048:(hc + 1) * 2048], dsrc(hc)),
                        0, ("wd", r) if hc == NHC - 1 else None)

        @blk.sync
        def _(sy):
            DS["dB"].issue(sy.dma_start(b_dkv[:, :], dkvb.ap()), 0)
            DS["dB"].issue(sy.dma_start(b_dq[:, :], dqb.ap()), 0)
            DS["dB"].issue(sy.dma_start(b_uk[:, :], ukb.ap()), 0)
            DS["dB"].issue(sy.dma_start(b_uq[:, :], uqb.ap()), 0)
            DS["dB"].issue(sy.dma_start(b_wo[:, :], wob.ap()), 0, "bias")
            for tt in range(NT):
                if tt >= 2:
                    sy.wait_ge(SM["vX"], tt - 1)
                    sy.wait_ge(SM["sXn"], tt - 1)
                DS["dX"].issue(sy.dma_start(xt[tt % 2][:, :],
                                            xin.ap()[tt * 128:(tt + 1) * 128, :]),
                               tt, ("x", tt))
            NU = NR * 64
            for u in range(NU):
                rd = u + 3
                if 64 <= rd < NU:
                    dc, tg = (rd % 64) // 4, rd % 4
                    DS["dRw"].wait(sy, ("rw", rd - 64))
                    sy.wait_ge(SM["vDn"], rd - 2)
                    DS["dRr"].issue(
                        sy.dma_start(rmwb[rd % 3][:, :],
                                     partial.ap()[dc * 128:(dc + 1) * 128,
                                                  tg * 512:(tg + 1) * 512]),
                        rd, ("rr", rd))
                dc, tg = (u % 64) // 4, u % 4
                sy.wait_ge(SM["vDn"], u + 1)
                DS["dRw"].issue(
                    sy.dma_start(partial.ap()[dc * 128:(dc + 1) * 128,
                                              tg * 512:(tg + 1) * 512],
                                 outb[u % 4][:, :]),
                    u, ("rw", u))
            for uu in range(NU - 4, NU):
                DS["dRw"].wait(sy, ("rw", uu))

        @blk.vector
        def _(v):
            v.wait_ge(SM["gI"], 3)
            v.tensor_single_scalar(ident[:, :], io1[:, :], ioc[:, :], ALU.is_equal
                                   ).then_inc(SM["vI"], 1)
            for tt in range(NT):
                p = tt % 2
                DS["dX"].wait(v, ("x", tt))
                if tt >= 2:
                    v.wait_ge(SM["sXn"], tt - 1)
                v.reduce_sum(sums[p][:, :], xt[p][:, :], axis=AX.X).then_inc(SM["vX"], 1)
                v.wait_ge(SM["vX"], tt + 1)
                v.wait_ge(SM["sSq"], tt + 1)
                v.wait_ge(SM["vW"], vw_count[0])
                vi(v.tensor_scalar_mul(stat[p][:, 0:1], sums[p][:, :], 1.0 / D))
                vi(v.tensor_scalar_mul(stat[p][:, 1:2], ssq[p][:, :], 1.0 / D))
                v.wait_ge(SM["vW"], vw_count[0])
                vi(v.tensor_mul(stat[p][:, 2:3], stat[p][:, 0:1], stat[p][:, 0:1]))
                v.wait_ge(SM["vW"], vw_count[0])
                v.tensor_scalar(stat[p][:, 4:5], stat[p][:, 1:2], stat[p][:, 2:3],
                                EPS, ALU.subtract, ALU.add).then_inc(SM["vVar"], 1)
                v.wait_ge(SM["sSd"], tt + 1)
                vi(v.reciprocal(stat[p][:, 5:6], sd[p][:, :]))
                v.wait_ge(SM["vW"], vw_count[0])
                v.scalar_tensor_tensor(stat[p][:, 6:7], stat[p][:, 0:1], -1.0,
                                       stat[p][:, 5:6], ALU.mult, ALU.mult
                                       ).then_inc(SM["vSt"], 1)
                for gidx in range(4):
                    G = 4 * tt + gidx
                    v.wait_ge(SM["tTr"], G + 1)
                    src = trp_of(G).rearrange("p (a b) -> p a b", a=4)
                    dst = xnT3[:, 4 * gidx:4 * gidx + 4, tt * 128:(tt + 1) * 128]
                    v.tensor_copy(dst, src).then_inc(SM["vTr"], 1)
            for tt in range(NT):
                p = tt % 2
                if tt == 0:
                    v.wait_ge(SM["tLg"], NT)
                v.reduce_max(negA[:, tt:tt + 1], lgp_of(tt), axis=AX.X, negate=True
                             ).then_inc(SM["vLa"], 1)
                v.wait_ge(SM["sLx"], tt + 1)
                vi(v.reciprocal(inv[:, :], sume[p][:, :]))
                v.wait_ge(SM["vW"], vw_count[0])
                vi(v.tensor_scalar_mul(prb[p][:, :], prb[p][:, :], inv[:, :]))
                v.wait_ge(SM["vW"], vw_count[0])
                vi(v.reduce_max(negb[:, :], prb[p][:, :], axis=AX.X, negate=True))
                v.wait_ge(SM["vW"], vw_count[0])
                vi(v.tensor_scalar_mul(negb[:, :], negb[:, :], -1.0))
                v.wait_ge(SM["vW"], vw_count[0])
                vi(v.tensor_single_scalar(lgw[:, :], prb[p][:, :], negb[:, :], ALU.is_ge))
                v.wait_ge(SM["vW"], vw_count[0])
                vi(v.tensor_sub(lgw[:, :], prb[p][:, :], lgw[:, :]))
                v.wait_ge(SM["vW"], vw_count[0])
                vi(v.reduce_max(negb[:, :], lgw[:, :], axis=AX.X, negate=True))
                v.wait_ge(SM["vW"], vw_count[0])
                vi(v.tensor_scalar_mul(negb[:, :], negb[:, :], -1.0))
                v.wait_ge(SM["vW"], vw_count[0])
                v.scalar_tensor_tensor(cmb[p][:, :], prb[p][:, :], negb[:, :],
                                       prb[p][:, :], ALU.is_ge, ALU.mult
                                       ).then_inc(SM["vCb"], 1)
                v.wait_ge(SM["tCT"], tt + 1)
                v.tensor_copy(combT[:, tt * 128:(tt + 1) * 128], ctp_of(p)[0:8, :]
                              ).then_inc(SM["vCT"], 1)
            for r in range(NR):
                if r == E:
                    v.memset(comb_bc[:, :], 1.0).then_inc(SM["vCbc"], 1)
                for hc in range(NHC):
                    for tg in range(4):
                        G = (r * NHC + hc) * 4 + tg
                        v.wait_ge(SM["tGb"], G + 1)
                        if r < E:
                            DS["dCb"].wait(v, ("cb", r))
                        else:
                            v.wait_ge(SM["vCbc"], 1)
                        v.tensor_tensor(upb[:, tg * 512:(tg + 1) * 512], ups[G % 2],
                                        comb_bc[:, tg * 512:(tg + 1) * 512], ALU.mult
                                        ).then_inc(SM["vT1"], 1)
                        v.wait_ge(SM["sSl"], G + 1)
                        v.wait_ge(SM["vT1"], G + 1)
                        v.tensor_tensor(
                            gpr[:, hc * 2048 + tg * 512: hc * 2048 + (tg + 1) * 512],
                            sgb[:, tg * 512:(tg + 1) * 512],
                            upb[:, tg * 512:(tg + 1) * 512], ALU.mult
                        ).then_inc(SM["vT2"], 1)
                for dc in range(ND):
                    for tg in range(4):
                        u = r * 64 + dc * 4 + tg
                        v.wait_ge(SM["tDn"], u + 1)
                        if u >= 4:
                            DS["dRw"].wait(v, ("rw", u - 4))
                        if r == 0:
                            v.tensor_copy(outb[u % 4][:, :], dps[u % 2]).then_inc(SM["vDn"], 1)
                        else:
                            DS["dRr"].wait(v, ("rr", u))
                            v.tensor_tensor(outb[u % 4][:, :], dps[u % 2], rmwb[u % 3][:, :],
                                            ALU.add).then_inc(SM["vDn"], 1)

        @blk.scalar
        def _(s):
            def sq(tt):
                p = tt % 2
                DS["dX"].wait(s, ("x", tt))
                if tt >= 2:
                    s.wait_ge(SM["tTr"], 4 * (tt - 1))
                    s.wait_ge(SM["vSt"], tt - 1)
                s.activation(xnm[p][:, :], xt[p][:, :], AF.Square,
                             accum_out=ssq[p][:, :]).then_inc(SM["sSq"], 1)

            sq(0)
            for tt in range(NT):
                p = tt % 2
                if tt + 1 < NT:
                    sq(tt + 1)
                s.wait_ge(SM["vVar"], tt + 1)
                s.activation(sd[p][:, :], stat[p][:, 4:5], AF.Sqrt).then_inc(SM["sSd"], 1)
                s.wait_ge(SM["vSt"], tt + 1)
                s.activation(xnm[p][:, :], xt[p][:, :], AF.Identity,
                             bias=stat[p][:, 6:7], scale=stat[p][:, 5:6]
                             ).then_inc(SM["sXn"], 1)
            for tt in range(NT):
                p = tt % 2
                s.wait_ge(SM["vLa"], tt + 1)
                if tt >= 2:
                    s.wait_ge(SM["vCb"], tt - 1)
                s.activation(prb[p][:, :], lgp_of(tt), AF.Exp, bias=negA[:, tt:tt + 1],
                             accum_out=sume[p][:, :]).then_inc(SM["sLx"], 1)
            for r in range(NR):
                for hc in range(NHC):
                    for tg in range(4):
                        G = (r * NHC + hc) * 4 + tg
                        s.wait_ge(SM["tGa"], G + 1)
                        if G >= 4:
                            s.wait_ge(SM["vT2"], G - 3)
                        s.activation(sgb[:, tg * 512:(tg + 1) * 512], gps[G % 2], AF.Silu
                                     ).then_inc(SM["sSl"], 1)

        @blk.tensor
        def _(t):
            t.wait_ge(SM["vI"], 1)
            for tt in range(NT):
                p = tt % 2
                for gidx in range(4):
                    G = 4 * tt + gidx
                    if gidx == 0:
                        t.wait_ge(SM["sXn"], tt + 1)
                    if G >= 2:
                        t.wait_ge(SM["vTr"], G - 1)
                    trp = trp_of(G)
                    for j in range(4):
                        inst = t.transpose(
                            trp[:, j * 128:(j + 1) * 128],
                            xnm[tt % 2][:, (4 * gidx + j) * 128:(4 * gidx + j + 1) * 128],
                            ident[:, :])
                    inst.then_inc(SM["tTr"], 1)
                if tt == 0:
                    DS["dG"].wait(t, "setup")
                t.wait_ge(SM["vTr"], 4 * (tt + 1))
                for dc in range(ND):
                    inst = t.matmul(lgp_of(tt), xnT3[:, dc, tt * 128:(tt + 1) * 128],
                                    gwv[:, dc * 8:(dc + 1) * 8],
                                    start=(dc == 0), stop=(dc == ND - 1))
                inst.then_inc(SM["tLg"], 1)
            for tt in range(NT):
                p = tt % 2
                t.wait_ge(SM["vCb"], tt + 1)
                if tt >= 2:
                    t.wait_ge(SM["vCT"], tt - 1)
                t.transpose(ctp_of(p)[0:8, :], cmb[p][:, :], ident[:, :]).then_inc(SM["tCT"], 1)
            for r in range(NR):
                for hc in range(NHC):
                    c = r * NHC + hc
                    for tg in range(4):
                        G = c * 4 + tg
                        if tg == 0:
                            DS["dWg"].wait(t, ("gu", c))
                        if G >= 2:
                            t.wait_ge(SM["sSl"], G - 1)
                        for dc in range(ND):
                            inst = t.matmul(gps[G % 2], gw[c % 2][:, dc * 128:(dc + 1) * 128],
                                            xnT3[:, dc, tg * 512:(tg + 1) * 512],
                                            start=(dc == 0), stop=(dc == ND - 1))
                        inst.then_inc(SM["tGa"], 1)
                        if G >= 2:
                            t.wait_ge(SM["vT1"], G - 1)
                        for dc in range(ND):
                            inst = t.matmul(ups[G % 2], uw[c % 2][:, dc * 128:(dc + 1) * 128],
                                            xnT3[:, dc, tg * 512:(tg + 1) * 512],
                                            start=(dc == 0), stop=(dc == ND - 1))
                        inst.then_inc(SM["tGb"], 1)
                for dc in range(ND):
                    for tg in range(4):
                        u = r * 64 + dc * 4 + tg
                        if dc == 0 and tg == 0:
                            DS["dWd"].wait(t, ("wd", r))
                            t.wait_ge(SM["vT2"], 44 * (r + 1))
                        if u >= 2:
                            t.wait_ge(SM["vDn"], u - 1)
                        for hc in range(NHC):
                            inst = t.matmul(
                                dps[u % 2],
                                wdb[:, hc * 2048 + dc * 128: hc * 2048 + (dc + 1) * 128],
                                gpr[:, hc * 2048 + tg * 512: hc * 2048 + (tg + 1) * 512],
                                start=(hc == 0), stop=(hc == NHC - 1))
                        inst.then_inc(SM["tDn"], 1)

    st.close()

    st = ExitStack()
    ckv = st.enter_context(nc.sbuf_tensor("ckv", [128, NCC * S], BF16, side="right"))
    cq = st.enter_context(nc.sbuf_tensor("cq", [128, NQC * S], BF16, side="right"))
    st3 = ExitStack()
    dkvw = [st3.enter_context(nc.sbuf_tensor(f"dkvw{p}", [128, 2048], BF16, side="right"))
            for p in range(2)]
    dqw = [st3.enter_context(nc.sbuf_tensor(f"dqw{p}", [128, 2048], BF16, side="right"))
           for p in range(2)]

    ckv3 = ckv.ap().rearrange("p (cc t) -> p cc t", cc=NCC)
    cq3 = cq.ap().rearrange("p (qc t) -> p qc t", qc=NQC)
    pjp = [B[:, 1024:1536], B[:, 1536:2048]]
    wotA = nc.alloc_sbuf_tensor_at("wotA", [128, 12 * 2048], BF16,
                                   offset=nc.lookup_mloc(cq).addr)
    wotB = nc.alloc_sbuf_tensor_at("wotB", [128, 4 * 2048], BF16,
                                   offset=nc.lookup_mloc(ckv).addr)

    with nc.Block() as blk:
        @blk.gpsimd
        def _(g):
            for cc in range(NCC):
                if cc >= 2:
                    g.wait_ge(SM["tPj"], 4 * (cc - 1))
                DS["dPw"].issue(g.dma_start(dkvw[cc % 2][:, :], dkvT.ap()[cc]),
                                cc, ("dkv", cc))
            for qc in range(NQC):
                if qc >= 2:
                    g.wait_ge(SM["tPj"], 16 + 4 * (qc - 1))
                DS["dQw"].issue(g.dma_start(dqw[qc % 2][:, :], dqT.ap()[qc]),
                                qc, ("dq", qc))

        @blk.tensor
        def _(t):
            for cc in range(NCC):
                for tg in range(4):
                    j = cc * 4 + tg
                    if tg == 0:
                        DS["dPw"].wait(t, ("dkv", cc))
                    if j >= 2:
                        t.wait_ge(SM["sPj"], j - 1)
                    for dc in range(ND):
                        inst = t.matmul(pjp[j % 2], dkvw[cc % 2][:, dc * 128:(dc + 1) * 128],
                                        xnT3[:, dc, tg * 512:(tg + 1) * 512],
                                        start=(dc == 0), stop=(dc == ND - 1))
                    inst.then_inc(SM["tPj"], 1)
            for qc in range(NQC):
                for tg in range(4):
                    j = 16 + qc * 4 + tg
                    if tg == 0:
                        DS["dQw"].wait(t, ("dq", qc))
                    t.wait_ge(SM["sPj"], j - 1)
                    for dc in range(ND):
                        inst = t.matmul(pjp[j % 2], dqw[qc % 2][:, dc * 128:(dc + 1) * 128],
                                        xnT3[:, dc, tg * 512:(tg + 1) * 512],
                                        start=(dc == 0), stop=(dc == ND - 1))
                    inst.then_inc(SM["tPj"], 1)

        @blk.scalar
        def _(s):
            DS["dB"].wait(s, "bias")
            for cc in range(NCC):
                for tg in range(4):
                    j = cc * 4 + tg
                    s.wait_ge(SM["tPj"], j + 1)
                    s.activation(ckv3[:, cc, tg * 512:(tg + 1) * 512], pjp[j % 2], AF.Identity,
                                 bias=b_dkv[:, cc:cc + 1]).then_inc(SM["sPj"], 1)
            for qc in range(NQC):
                for tg in range(4):
                    j = 16 + qc * 4 + tg
                    s.wait_ge(SM["tPj"], j + 1)
                    s.activation(cq3[:, qc, tg * 512:(tg + 1) * 512], pjp[j % 2], AF.Identity,
                                 bias=b_dq[:, qc:qc + 1]).then_inc(SM["sPj"], 1)

    st3.close()

    kh = st.enter_context(nc.sbuf_tensor("kh", [128, S], BF16, side="right"))
    qh = st.enter_context(nc.sbuf_tensor("qh", [128, S], BF16, side="right"))
    vh = [st.enter_context(nc.sbuf_tensor(f"vh{p}", [128, NT * 132], BF16, side="right"))
          for p in range(2)]
    atm = st.enter_context(nc.sbuf_tensor("atm", [128, S], BF16, side="right"))
    ukw = [st.enter_context(nc.sbuf_tensor("ukw0", [128, 512], BF16, side="right"))]
    uvw = [st.enter_context(nc.sbuf_tensor("uvw0", [128, 512], BF16, side="right"))]
    uqw = [st.enter_context(nc.sbuf_tensor("uqw0", [128, 1536], BF16, side="right"))]
    pt = [st.enter_context(nc.sbuf_tensor(f"pt{p}", [128, 16 * 512], BF16, side="right"))
          for p in range(3)]

    vh3 = [vv.ap().rearrange("p (kt c) -> p kt c", kt=NT) for vv in vh]
    avp = [B[:, 0:512], B[:, 512:1024], B[:, 1024:1536], B[:, 1536:2048]]
    attnT = xnT
    attnT3 = xnT3
    pj4 = [A[:, 1024:1536], A[:, 1536:2048]]

    def trp4_of(gt):
        return (A[:, 0:256] if gt % 2 == 0 else A[:, 512:768]).bitcast(BF16)

    def proj_consumer(ju):
        h, idx = ju // 24, ju % 24
        if idx < 4:
            return ("sKQ", 8 * h + idx + 1)
        if idx < 8:
            return ("sKQ", 8 * h + idx + 1)
        return ("vV", 16 * h + (idx - 8) + 1)

    with nc.Block() as blk:
        @blk.gpsimd
        def _(g):
            g.memset(vh3[0][:, :, 128:129], 1.0).then_inc(SM["gOn"], 1)
            g.memset(vh3[1][:, :, 128:129], 1.0).then_inc(SM["gOn"], 1)
            for h in range(H):
                if h == 0:
                    g.wait_ge(SM["tPj"], 64)
                if h >= 1:
                    g.wait_ge(SM["tKQ"], 24 * h)
                DS["dHw"].issue(g.dma_start(ukw[0][:, :], ukT.ap()[h]), h)
                DS["dHw"].issue(g.dma_start(uvw[0][:, :], uvT.ap()[h]), h)
                DS["dHw"].issue(g.dma_start(uqw[0][:, :], uqT.ap()[h]), h, ("h", h))
            g.wait_ge(SM["tKQ"], 24 * H)
            for hc in range(ND):
                dst = wotA[:, hc * 2048:(hc + 1) * 2048] if hc < 12 else \
                    wotB[:, (hc - 12) * 2048:(hc - 12 + 1) * 2048]
                DS["dWo"].issue(g.dma_start(dst, woT.ap()[hc]), hc, ("wo", hc))

        @blk.tensor
        def _(t):
            for h in range(H):
                DS["dHw"].wait(t, ("h", h))
                if h == 0:
                    t.wait_ge(SM["gOn"], 2)
                    t.wait_ge(SM["sPj"], 64)
                for idx in range(24):
                    ju = 24 * h + idx
                    if idx == 0 and h >= 1:
                        t.wait_ge(SM["sEx"], 16 * h)
                    if ju >= 2:
                        sem, val = proj_consumer(ju - 2)
                        t.wait_ge(SM[sem], val)
                    pp = pj4[ju % 2]
                    if idx < 4:
                        tg = idx
                        for cc in range(NCC):
                            inst = t.matmul(pp, ukw[0][:, cc * 128:(cc + 1) * 128],
                                            ckv3[:, cc, tg * 512:(tg + 1) * 512],
                                            start=(cc == 0), stop=(cc == NCC - 1))
                    elif idx < 8:
                        tg = idx - 4
                        for qc in range(NQC):
                            inst = t.matmul(pp, uqw[0][:, qc * 128:(qc + 1) * 128],
                                            cq3[:, qc, tg * 512:(tg + 1) * 512],
                                            start=(qc == 0), stop=(qc == NQC - 1))
                    else:
                        kt = idx - 8
                        for cc in range(NCC):
                            inst = t.matmul(pp[:, 0:128],
                                            ckv3[:, cc, kt * 128:(kt + 1) * 128],
                                            uvw[0][:, cc * 128:(cc + 1) * 128],
                                            start=(cc == 0), stop=(cc == NCC - 1))
                    inst.then_inc(SM["tKQ"], 1)
                def av_group(gq, qsub):
                    hh, qgg = gq // 4, gq % 4
                    q16 = gq * 4 + qsub
                    if qsub == 0:
                        t.wait_ge(SM["sEx"], gq * 4 + 4)
                        if qgg == 0:
                            t.wait_ge(SM["vV"], 16 * (hh + 1))
                    if q16 >= 4:
                        t.wait_ge(SM["vAv"], 2 * (q16 - 3))
                    for kt in range(NT):
                        inst = t.matmul(
                            avp[q16 % 4][:, 0:129],
                            pt[gq % 3][:, kt * 512 + qsub * 128:
                                       kt * 512 + (qsub + 1) * 128],
                            vh3[hh % 2][:, kt, 0:129],
                            start=(kt == 0), stop=(kt == NT - 1))
                    inst.then_inc(SM["tAV"], 1)

                def transposes_of(hh, sex_done):
                    t.wait_ge(SM["sEx"], sex_done)
                    for gidx in range(4):
                        gt = 4 * hh + gidx
                        t.wait_ge(SM["vAv"], 2 * (hh * 16 + 4 * (gidx + 1)))
                        if gt >= 2:
                            t.wait_ge(SM["vAT"], gt - 1)
                        trp = trp4_of(gt)
                        for j in range(4):
                            inst = t.transpose(
                                trp[:, j * 128:(j + 1) * 128],
                                atm[:, (4 * gidx + j) * 128:(4 * gidx + j + 1) * 128],
                                ident[:, :])
                        inst.then_inc(SM["tAT"], 1)

                for qg in range(4):
                    gq = h * 4 + qg
                    base16 = gq * 16
                    for kt in range(NT):
                        if kt == 0 and qg == 0:
                            t.wait_ge(SM["sKQ"], 8 * (h + 1))
                        if kt == 0 and qg == 1 and h >= 1:
                            t.wait_ge(SM["vAT"], 4 * h)
                        if qg == 0 and kt == 2:
                            t.wait_ge(SM["vV"], 16 * h + 15)
                        if qg == 0 and kt == 3:
                            t.wait_ge(SM["vV"], 16 * h + 16)
                        if base16 + kt >= 4:
                            gl = base16 + kt - 4
                            t.wait_ge(SM["sEx"], gl // 4 + 1)
                        t.matmul(A[:, (kt % 4) * 512:(kt % 4 + 1) * 512],
                                 kh[:, kt * 128:(kt + 1) * 128],
                                 qh[:, qg * 512:(qg + 1) * 512],
                                 start=True, stop=True).then_inc(SM["tS"], 1)
                        if kt % 4 == 3 and gq >= 1:
                            av_group(gq - 1, kt // 4)
                            if gq % 4 == 0 and kt == 15:
                                transposes_of(h - 1, 4 * (4 * h) + 4)
                if h == H - 1:
                    for qsub in range(4):
                        av_group(4 * H - 1, qsub)
                    transposes_of(H - 1, 16 * H)

        @blk.scalar
        def _(s):
            DS["dB"].wait(s, "bias")
            for h in range(H):
                for qg in range(4):
                    gq = h * 4 + qg
                    base16 = gq * 16
                    for j in range(4):
                        s.wait_ge(SM["tS"], base16 + 4 * j + 4)
                        if j == 0 and gq >= 3:
                            s.wait_ge(SM["tAV"], (gq - 2) * 4)
                        s.activation(pt[gq % 3][:, (4 * j) * 512:(4 * j + 4) * 512],
                                     A[:, 0:2048],
                                     AF.Exp, scale=SCALE).then_inc(SM["sEx"], 1)

        @blk.vector
        def _(v):
            DS["dG"].wait(v, "setup")

            def attn_evacs(hh, qlo, qhi):
                for qs in range(qlo, qhi):
                    q16 = hh * 16 + qs
                    v.wait_ge(SM["tAV"], q16 + 1)
                    if q16 >= 1:
                        v.wait_ge(SM["vAv"], 2 * q16)
                    vi(v.reciprocal(inv[:, :], avp[q16 % 4][:, 128:129]))
                    v.wait_ge(SM["vW"], vw_count[0])
                    v.tensor_scalar_mul(atm[:, qs * 128:(qs + 1) * 128],
                                        avp[q16 % 4][:, 0:128], inv[:, :]
                                        ).then_inc(SM["vAv"], 2)

            def attnT_evacs(hh):
                for gidx in range(4):
                    gt = 4 * hh + gidx
                    v.wait_ge(SM["tAT"], gt + 1)
                    v.tensor_copy(attnT3[:, hh, gidx * 512:(gidx + 1) * 512], trp4_of(gt)
                                  ).then_inc(SM["vAT"], 1)

            for h in range(H):
                for idx in range(8):
                    ju = 24 * h + idx
                    v.wait_ge(SM["tKQ"], ju + 1)
                    if idx < 4:
                        v.tensor_scalar_add(kh[:, idx * 512:(idx + 1) * 512], pj4[ju % 2],
                                            b_uk[:, h:h + 1]).then_inc(SM["sKQ"], 1)
                    else:
                        tg = idx - 4
                        v.tensor_scalar_add(qh[:, tg * 512:(tg + 1) * 512], pj4[ju % 2],
                                            b_uq[:, h:h + 1]).then_inc(SM["sKQ"], 1)
                for kt in range(NT):
                    ju = 24 * h + 8 + kt
                    if kt == 0 and h >= 2:
                        v.wait_ge(SM["tAV"], 16 * (h - 1))
                    v.wait_ge(SM["tKQ"], ju + 1)
                    v.tensor_tensor(vh3[h % 2][:, kt, 0:128], pj4[ju % 2][:, 0:128],
                                    uvb_bc[:, h * 128:(h + 1) * 128], ALU.add
                                    ).then_inc(SM["vV"], 1)
                if h >= 1:
                    attn_evacs(h - 1, 12, 16)
                    attnT_evacs(h - 1)
                attn_evacs(h, 0, 12)
            attn_evacs(H - 1, 12, 16)
            attnT_evacs(H - 1)

    st.close()

    st = ExitStack()
    prt = [st.enter_context(nc.sbuf_tensor(f"prt{p}", [128, 512], F32, side="left"))
           for p in range(2)]
    xtb = [st.enter_context(nc.sbuf_tensor(f"xtb{p}", [128, 512], F32, side="left"))
           for p in range(2)]
    tmpb = [st.enter_context(nc.sbuf_tensor("tmpb0", [128, 512], F32, side="left"))]
    fout = [st.enter_context(nc.sbuf_tensor(f"fout{p}", [128, 512], F32, side="left"))
            for p in range(2)]

    def wot_tile(hc, oc):
        if hc < 12:
            return wotA[:, hc * 2048 + oc * 128: hc * 2048 + (oc + 1) * 128]
        return wotB[:, (hc - 12) * 2048 + oc * 128: (hc - 12) * 2048 + (oc + 1) * 128]

    wops = [A[:, 0:512], A[:, 512:1024], A[:, 1024:1536], A[:, 1536:2048]]

    with nc.Block() as blk:
        @blk.sync
        def _(sy):
            for uu in range(NR * 64 - 4, NR * 64):
                DS["dRw"].wait(sy, ("rw", uu))
            for j in range(-2, 64):
                rd = j + 2
                if rd < 64:
                    oc, tg = rd // 4, rd % 4
                    if rd >= 2:
                        sy.wait_ge(SM["vWo"], 2 * (rd - 2) + 2)
                    DS["dPr"].issue(
                        sy.dma_start(prt[rd % 2][:, :],
                                     partial.ap()[oc * 128:(oc + 1) * 128,
                                                  tg * 512:(tg + 1) * 512]),
                        rd, ("pr", rd))
                    DS["dXr"].issue(
                        sy.dma_start(xtb[rd % 2][:, :],
                                     xTd.ap()[oc * 128:(oc + 1) * 128,
                                              tg * 512:(tg + 1) * 512]),
                        rd, ("xr", rd))
                if j >= 0:
                    oc, tg = j // 4, j % 4
                    sy.wait_ge(SM["vWo"], 2 * j + 2)
                    DS["dOw"].issue(
                        sy.dma_start(outT.ap()[oc * 128:(oc + 1) * 128,
                                               tg * 512:(tg + 1) * 512],
                                     fout[j % 2][:, :]),
                        j, ("ow", j))
            DS["dOw"].wait(sy, ("ow", 62))
            DS["dOw"].wait(sy, ("ow", 63))

        @blk.tensor
        def _(t):
            t.wait_ge(SM["vAT"], 4 * H)
            for j in range(64):
                oc, tg = j // 4, j % 4
                if j >= 4:
                    t.wait_ge(SM["vWo"], 2 * (j - 4) + 2)
                for hc in range(ND):
                    if j == 0:
                        DS["dWo"].wait(t, ("wo", hc))
                    elif j == 1 and hc == 0:
                        DS["dWo"].wait(t, ("wo", ND - 1))
                    inst = t.matmul(wops[j % 4], wot_tile(hc, oc),
                                    attnT3[:, hc, tg * 512:(tg + 1) * 512],
                                    start=(hc == 0), stop=(hc == ND - 1))
                inst.then_inc(SM["tWo"], 1)

        @blk.vector
        def _(v):
            DS["dB"].wait(v, "bias")
            for j in range(64):
                oc = j // 4
                v.wait_ge(SM["tWo"], j + 1)
                DS["dPr"].wait(v, ("pr", j))
                DS["dXr"].wait(v, ("xr", j))
                if j >= 1:
                    v.wait_ge(SM["vWo"], 2 * j)
                vi(v.scalar_tensor_tensor(tmpb[0][:, :], wops[j % 4], b_wo[:, oc:oc + 1],
                                          prt[j % 2][:, :], ALU.add, ALU.add))
                if j >= 2:
                    DS["dOw"].wait(v, ("ow", j - 2))
                v.wait_ge(SM["vW"], vw_count[0])
                v.tensor_add(fout[j % 2][:, :], tmpb[0][:, :], xtb[j % 2][:, :]
                             ).then_inc(SM["vWo"], 2)

    st.close()
    return nc



def _prep_weights(inputs):
    f32 = np.float32

    def C(a):
        return np.ascontiguousarray(a, dtype=f32)

    W = {}
    W["dkvT"] = C(inputs["L0_attn_W_DKV_w"].reshape(NCC, 128, ND, 128).transpose(0, 3, 2, 1)
                  .reshape(NCC, 128, 2048))
    W["dqT"] = C(inputs["L0_attn_W_DQ_w"].reshape(NQC, 128, ND, 128).transpose(0, 3, 2, 1)
                 .reshape(NQC, 128, 2048))
    W["ukT"] = C(inputs["L0_attn_W_UK_w"].reshape(H, 128, NCC, 128).transpose(0, 3, 2, 1)
                 .reshape(H, 128, 512))
    W["uvT"] = C(inputs["L0_attn_W_UV_w"].reshape(H, 128, NCC, 128).transpose(0, 3, 2, 1)
                 .reshape(H, 128, 512))
    W["uqT"] = C(inputs["L0_attn_W_UQ_w"].reshape(H, 128, NQC, 128).transpose(0, 3, 2, 1)
                 .reshape(H, 128, 1536))
    W["woT"] = C(inputs["L0_attn_W_O_w"].reshape(ND, 128, ND, 128).transpose(2, 3, 0, 1)
                 .reshape(ND, 128, 2048))
    W["gwp"] = C(inputs["L0_ffn_gate_w"].reshape(ND, 128, 8).transpose(1, 0, 2)
                 .reshape(128, 128))
    W["wgp"] = C(inputs["L0_ffn_w_gate"].reshape(E, ND, 128, NHC, 128)
                 .transpose(0, 3, 2, 1, 4).reshape(E, NHC, 128, 2048))
    W["wup"] = C(inputs["L0_ffn_w_up"].reshape(E, ND, 128, NHC, 128)
                 .transpose(0, 3, 2, 1, 4).reshape(E, NHC, 128, 2048))
    W["wdp"] = C(inputs["L0_ffn_w_down"].reshape(E, NHC, 128, 2048))
    W["swgp"] = C(inputs["L0_ffn_sw_gate"].reshape(ND, 128, 22, 128).transpose(2, 1, 0, 3)
                  .reshape(22, 128, 2048))
    W["swup"] = C(inputs["L0_ffn_sw_up"].reshape(ND, 128, 22, 128).transpose(2, 1, 0, 3)
                  .reshape(22, 128, 2048))
    W["swdp"] = C(inputs["L0_ffn_sw_down"].reshape(22, 128, 2048))
    W["dkvb"] = C(inputs["L0_attn_W_DKV_b"].reshape(NCC, 128).T)
    W["dqb"] = C(inputs["L0_attn_W_DQ_b"].reshape(NQC, 128).T)
    W["ukb"] = C(inputs["L0_attn_W_UK_b"].reshape(H, 128).T)
    W["uqb"] = C(inputs["L0_attn_W_UQ_b"].reshape(H, 128).T)
    W["wob"] = C(inputs["L0_attn_W_O_b"].reshape(ND, 128).T)
    W["uvbr"] = C(inputs["L0_attn_W_UV_b"].reshape(1, 2048))
    return W


def kernel(**inputs):
    x = np.asarray(inputs["L0_attn_hidden_states"], dtype=np.float32)
    W = _prep_weights(inputs)
    nc = build_nc()
    in_maps = []
    for c in range(NCORES):
        m = dict(W)
        m["xin"] = np.ascontiguousarray(x[c])
        m["xTd"] = np.ascontiguousarray(x[c].T)
        in_maps.append(m)
    res = run_bass_kernel_spmd(nc, in_maps, core_ids=list(range(NCORES)))
    out = np.stack([np.asarray(res.results[c]["outT"], dtype=np.float32).T
                    for c in range(NCORES)], axis=0)
    return np.ascontiguousarray(out)


# version 15
# speedup vs baseline: 1.1287x; 1.0362x over previous
import math
from contextlib import ExitStack

import numpy as np

import concourse.bass as bass
import concourse.mybir as mybir
from concourse.bass_utils import run_bass_kernel_spmd

F32 = mybir.dt.float32
BF16 = mybir.dt.bfloat16
I32 = mybir.dt.int32
AF = mybir.ActivationFunctionType
ALU = mybir.AluOpType
AX = mybir.AxisListType

NCORES = 8
S = 2048
D = 2048
NT = S // 128
ND = D // 128
H = 16
KVC, QC = 512, 1536
NCC, NQC = KVC // 128, QC // 128
E, HF = 8, 1408
NHC = HF // 128
NR = E + 2
NTG = S // 512
SCALE = 1.0 / math.sqrt(128.0)
EPS = 1e-5


def build_nc():
    nc = bass.Bass("TRN2", target_bir_lowering=False)

    xin = nc.dram_tensor("xin", [S, D], F32, kind="ExternalInput")
    xTd = nc.dram_tensor("xTd", [D, S], F32, kind="ExternalInput")
    dkvT = nc.dram_tensor("dkvT", [NCC, 128, 2048], F32, kind="ExternalInput")
    dqT = nc.dram_tensor("dqT", [NQC, 128, 2048], F32, kind="ExternalInput")
    ukT = nc.dram_tensor("ukT", [H, 128, 512], F32, kind="ExternalInput")
    uvT = nc.dram_tensor("uvT", [H, 128, 512], F32, kind="ExternalInput")
    uqT = nc.dram_tensor("uqT", [H, 128, 1536], F32, kind="ExternalInput")
    woT = nc.dram_tensor("woT", [ND, 128, 2048], F32, kind="ExternalInput")
    gwp = nc.dram_tensor("gwp", [128, 128], F32, kind="ExternalInput")
    wgp = nc.dram_tensor("wgp", [E, NHC, 128, 2048], F32, kind="ExternalInput")
    wup = nc.dram_tensor("wup", [E, NHC, 128, 2048], F32, kind="ExternalInput")
    wdp = nc.dram_tensor("wdp", [E, NHC, 128, 2048], F32, kind="ExternalInput")
    swgp = nc.dram_tensor("swgp", [22, 128, 2048], F32, kind="ExternalInput")
    swup = nc.dram_tensor("swup", [22, 128, 2048], F32, kind="ExternalInput")
    swdp = nc.dram_tensor("swdp", [22, 128, 2048], F32, kind="ExternalInput")
    dkvb = nc.dram_tensor("dkvb", [128, NCC], F32, kind="ExternalInput")
    dqb = nc.dram_tensor("dqb", [128, NQC], F32, kind="ExternalInput")
    ukb = nc.dram_tensor("ukb", [128, H], F32, kind="ExternalInput")
    uqb = nc.dram_tensor("uqb", [128, H], F32, kind="ExternalInput")
    wob = nc.dram_tensor("wob", [128, ND], F32, kind="ExternalInput")
    uvbr = nc.dram_tensor("uvbr", [1, 2048], F32, kind="ExternalInput")

    outT = nc.dram_tensor("outT", [D, S], F32, kind="ExternalOutput")
    partial = nc.dram_tensor("partial", [D, S], F32, kind="Internal")
    combhbm = nc.dram_tensor("combhbm", [8, 2048], BF16, kind="Internal")

    ident = nc.alloc_sbuf_tensor("ident", [128, 128], BF16)
    onesc = nc.alloc_sbuf_tensor("onesc", [1, 128], BF16)
    gwv = nc.alloc_sbuf_tensor("gwv", [128, 128], BF16)
    uvb_bc = nc.alloc_sbuf_tensor("uvb_bc", [128, 2048], BF16)
    b_dkv = nc.alloc_sbuf_tensor("b_dkv", [128, NCC], F32)
    b_dq = nc.alloc_sbuf_tensor("b_dq", [128, NQC], F32)
    b_uk = nc.alloc_sbuf_tensor("b_uk", [128, H], F32)
    b_uq = nc.alloc_sbuf_tensor("b_uq", [128, H], F32)
    b_wo = nc.alloc_sbuf_tensor("b_wo", [128, ND], F32)
    stat = [nc.alloc_sbuf_tensor(f"stat{p}", [128, 16], F32) for p in range(2)]
    ssq = [nc.alloc_sbuf_tensor(f"ssq{p}", [128, 1], F32) for p in range(2)]
    sums = [nc.alloc_sbuf_tensor(f"sums{p}", [128, 1], F32) for p in range(2)]
    sd = [nc.alloc_sbuf_tensor(f"sd{p}", [128, 1], F32) for p in range(2)]
    negm = [nc.alloc_sbuf_tensor(f"negm{p}", [128, 1], F32) for p in range(2)]
    sume = [nc.alloc_sbuf_tensor(f"sume{p}", [128, 1], F32) for p in range(2)]
    prb = [nc.alloc_sbuf_tensor(f"prb{p}", [128, 8], F32) for p in range(2)]
    cmb = [nc.alloc_sbuf_tensor(f"cmb{p}", [128, 8], BF16) for p in range(2)]
    lgw = nc.alloc_sbuf_tensor("lgw", [128, 8], F32)
    negb = nc.alloc_sbuf_tensor("negb", [128, 1], F32)
    negA = nc.alloc_sbuf_tensor("negA", [128, NT], F32)
    inv = nc.alloc_sbuf_tensor("inv", [128, 1], F32)

    xnT = nc.alloc_sbuf_tensor("xnT", [128, ND * S], BF16)

    A = nc.alloc_psum_tensor("A", [128, 2048], F32)
    B = nc.alloc_psum_tensor("B", [128, 2048], F32)

    sem_names = [
        "gI", "vI",
        "sSq", "sSd", "sXn", "vX", "vVar", "vSt", "tTr", "vTr",
        "tLg", "vLa", "sLx", "vCb", "tCT", "vCT", "vW",
        "tGa", "tGb", "sSl", "vT1", "vT2",
        "tCb", "vCbc", "tDn", "vDn",
        "tPj", "sPj",
        "gOn", "tKQ", "sKQ", "vV", "tS", "sEx", "tAV", "vAv", "tAT", "vAT",
        "tWo", "vWo",
    ]
    SM = {n: nc.alloc_semaphore(n) for n in sem_names}
    vw_count = [0]

    class DmaStream:

        def __init__(self, name, nslots):
            self.sems = [nc.alloc_semaphore(f"{name}_{i}") for i in range(nslots)]
            self.tot = [0] * nslots
            self.mark = {}

        def issue(self, inst, slot, label=None):
            k = slot % len(self.sems)
            inst.then_inc(self.sems[k], 16)
            self.tot[k] += 16
            if label is not None:
                self.mark[label] = (self.sems[k], self.tot[k])

        def wait(self, eng, label):
            s, v = self.mark[label]
            eng.wait_ge(s, v)

    DS = {
        "dB": DmaStream("dB", 1), "dG": DmaStream("dG", 1),
        "dX": DmaStream("dX", 2),
        "dWg": DmaStream("dWg", 2), "dWd": DmaStream("dWd", 1),
        "dCw": DmaStream("dCw", 1), "dCb": DmaStream("dCb", 1),
        "dRr": DmaStream("dRr", 3), "dRw": DmaStream("dRw", 4),
        "dPw": DmaStream("dPw", 2), "dQw": DmaStream("dQw", 2), "dHw": DmaStream("dHw", 2),
        "dHk": DmaStream("dHk", 1), "dHq": DmaStream("dHq", 1), "dHv": DmaStream("dHv", 1),
        "dWo": DmaStream("dWo", 16),
        "dPr": DmaStream("dPr", 2), "dXr": DmaStream("dXr", 2),
        "dOw": DmaStream("dOw", 2),
    }

    def vi(inst):
        inst.then_inc(SM["vW"], 1)
        vw_count[0] += 1
        return inst

    xnT3 = xnT.ap().rearrange("p (dc t) -> p dc t", dc=ND)

    st = ExitStack()
    gw = [st.enter_context(nc.sbuf_tensor(f"gw{p}", [128, 2048], BF16, side="right"))
          for p in range(2)]
    uw = [st.enter_context(nc.sbuf_tensor(f"uw{p}", [128, 2048], BF16, side="right"))
          for p in range(2)]
    sgb = st.enter_context(nc.sbuf_tensor("sgb", [128, 2048], BF16, side="right"))
    upb = st.enter_context(nc.sbuf_tensor("upb", [128, 2048], BF16, side="right"))
    comb_bc = st.enter_context(nc.sbuf_tensor("comb_bc", [128, 2048], BF16, side="right"))
    gpr = st.enter_context(nc.sbuf_tensor("gpr", [128, NHC * 2048], BF16, side="right"))
    wdb = st.enter_context(nc.sbuf_tensor("wdb", [128, NHC * 2048], BF16, side="right"))
    rmwb = [st.enter_context(nc.sbuf_tensor(f"rmwb{p}", [128, 512], F32, side="right"))
            for p in range(3)]
    outb = [st.enter_context(nc.sbuf_tensor(f"outb{p}", [128, 512], F32, side="right"))
            for p in range(4)]
    wdb_addr = nc.lookup_mloc(wdb).addr
    combT = nc.alloc_sbuf_tensor_at("combT", [8, S], BF16, offset=wdb_addr + 40960)
    xt = [nc.alloc_sbuf_tensor_at(f"xt{p}", [128, 2048], F32, offset=wdb_addr + p * 8192)
          for p in range(2)]
    xnm = [nc.alloc_sbuf_tensor_at(f"xnm{p}", [128, 2048], BF16,
                                   offset=wdb_addr + 16384 + p * 4096)
           for p in range(2)]
    sgb_addr = nc.lookup_mloc(sgb).addr
    io1 = nc.alloc_sbuf_tensor_at("io1", [128, 128], F32, offset=sgb_addr)
    ioc = nc.alloc_sbuf_tensor_at("ioc", [128, 1], F32, offset=sgb_addr + 512)

    def lgp_of(tt):
        return A[:, 1024 + 8 * tt: 1024 + 8 * (tt + 1)]

    def trp_of(G):
        return (A[:, 0:256] if G % 2 == 0 else A[:, 512:768]).bitcast(BF16)

    def ctp_of(p):
        return (B[:, 0:64] if p == 0 else B[:, 512:576]).bitcast(BF16)

    gps = [A[:, 0:512], A[:, 512:1024]]
    ups = [A[:, 1024:1536], A[:, 1536:2048]]
    dps = [B[:, 0:512], B[:, 512:1024]]

    def wsrc(r):
        if r < E:
            return (lambda hc: wgp.ap()[r, hc], lambda hc: wup.ap()[r, hc],
                    lambda hc: wdp.ap()[r, hc])
        half = r - E
        return (lambda hc: swgp.ap()[half * NHC + hc], lambda hc: swup.ap()[half * NHC + hc],
                lambda hc: swdp.ap()[half * NHC + hc])

    with nc.Block() as blk:
        @blk.gpsimd
        def _(g):
            g.memset(onesc[:, :], 1.0).then_inc(SM["gI"], 1)
            g.iota(io1[:, :], [[1, 128]], base=0, channel_multiplier=0,
                   allow_small_or_imprecise_dtypes=True).then_inc(SM["gI"], 1)
            g.iota(ioc[:, :], [[0, 1]], base=0, channel_multiplier=1,
                   allow_small_or_imprecise_dtypes=True).then_inc(SM["gI"], 1)
            DS["dG"].issue(g.dma_start(gwv[:, :], gwp.ap()), 0)
            DS["dG"].issue(g.dma_start(uvb_bc[:, :],
                                       uvbr.ap().partition_broadcast(128)[:, :]), 0, "setup")
            for r in range(NR):
                gsrc, usrc, dsrc = wsrc(r)
                if r < E:
                    if r == 0:
                        g.wait_ge(SM["vCT"], NT)
                        DS["dCw"].issue(g.dma_start(combhbm.ap(), combT[:, :]), 0, "cw")
                        DS["dCw"].wait(g, "cw")
                    else:
                        g.wait_ge(SM["vT1"], 44 * r)
                    DS["dCb"].issue(
                        g.dma_start(comb_bc[:, :],
                                    combhbm.ap()[r:r + 1, :].partition_broadcast(128)[:, :]),
                        0, ("cb", r))
                for hc in range(NHC):
                    c = r * NHC + hc
                    if c >= 2:
                        g.wait_ge(SM["tGa"], 4 * (c - 1))
                        g.wait_ge(SM["tGb"], 4 * (c - 1))
                    DS["dWg"].issue(g.dma_start(gw[c % 2][:, :], gsrc(hc)), c)
                    DS["dWg"].issue(g.dma_start(uw[c % 2][:, :], usrc(hc)), c, ("gu", c))
                    if r == 0:
                        if hc == 0:
                            g.wait_ge(SM["sXn"], NT)
                            g.wait_ge(SM["vX"], NT)
                            g.wait_ge(SM["tTr"], 4 * NT)
                    else:
                        g.wait_ge(SM["tDn"], 64 * r)
                    DS["dWd"].issue(
                        g.dma_start(wdb[:, hc * 2048:(hc + 1) * 2048], dsrc(hc)),
                        0, ("wd", r) if hc == NHC - 1 else None)

        @blk.sync
        def _(sy):
            DS["dB"].issue(sy.dma_start(b_dkv[:, :], dkvb.ap()), 0)
            DS["dB"].issue(sy.dma_start(b_dq[:, :], dqb.ap()), 0)
            DS["dB"].issue(sy.dma_start(b_uk[:, :], ukb.ap()), 0)
            DS["dB"].issue(sy.dma_start(b_uq[:, :], uqb.ap()), 0)
            DS["dB"].issue(sy.dma_start(b_wo[:, :], wob.ap()), 0, "bias")
            for tt in range(NT):
                if tt >= 2:
                    sy.wait_ge(SM["vX"], tt - 1)
                    sy.wait_ge(SM["sXn"], tt - 1)
                DS["dX"].issue(sy.dma_start(xt[tt % 2][:, :],
                                            xin.ap()[tt * 128:(tt + 1) * 128, :]),
                               tt, ("x", tt))
            NU = NR * 64
            for u in range(NU):
                rd = u + 3
                if 64 <= rd < NU:
                    dc, tg = (rd % 64) // 4, rd % 4
                    DS["dRw"].wait(sy, ("rw", rd - 64))
                    sy.wait_ge(SM["vDn"], rd - 2)
                    DS["dRr"].issue(
                        sy.dma_start(rmwb[rd % 3][:, :],
                                     partial.ap()[dc * 128:(dc + 1) * 128,
                                                  tg * 512:(tg + 1) * 512]),
                        rd, ("rr", rd))
                dc, tg = (u % 64) // 4, u % 4
                sy.wait_ge(SM["vDn"], u + 1)
                DS["dRw"].issue(
                    sy.dma_start(partial.ap()[dc * 128:(dc + 1) * 128,
                                              tg * 512:(tg + 1) * 512],
                                 outb[u % 4][:, :]),
                    u, ("rw", u))
            for uu in range(NU - 4, NU):
                DS["dRw"].wait(sy, ("rw", uu))

        @blk.vector
        def _(v):
            v.wait_ge(SM["gI"], 3)
            v.tensor_single_scalar(ident[:, :], io1[:, :], ioc[:, :], ALU.is_equal
                                   ).then_inc(SM["vI"], 1)
            for tt in range(NT):
                p = tt % 2
                DS["dX"].wait(v, ("x", tt))
                if tt >= 2:
                    v.wait_ge(SM["sXn"], tt - 1)
                v.reduce_sum(sums[p][:, :], xt[p][:, :], axis=AX.X).then_inc(SM["vX"], 1)
                v.wait_ge(SM["vX"], tt + 1)
                v.wait_ge(SM["sSq"], tt + 1)
                v.wait_ge(SM["vW"], vw_count[0])
                vi(v.tensor_scalar_mul(stat[p][:, 0:1], sums[p][:, :], 1.0 / D))
                vi(v.tensor_scalar_mul(stat[p][:, 1:2], ssq[p][:, :], 1.0 / D))
                v.wait_ge(SM["vW"], vw_count[0])
                vi(v.tensor_mul(stat[p][:, 2:3], stat[p][:, 0:1], stat[p][:, 0:1]))
                v.wait_ge(SM["vW"], vw_count[0])
                v.tensor_scalar(stat[p][:, 4:5], stat[p][:, 1:2], stat[p][:, 2:3],
                                EPS, ALU.subtract, ALU.add).then_inc(SM["vVar"], 1)
                v.wait_ge(SM["sSd"], tt + 1)
                vi(v.reciprocal(stat[p][:, 5:6], sd[p][:, :]))
                v.wait_ge(SM["vW"], vw_count[0])
                v.scalar_tensor_tensor(stat[p][:, 6:7], stat[p][:, 0:1], -1.0,
                                       stat[p][:, 5:6], ALU.mult, ALU.mult
                                       ).then_inc(SM["vSt"], 1)
                for gidx in range(4):
                    G = 4 * tt + gidx
                    v.wait_ge(SM["tTr"], G + 1)
                    src = trp_of(G).rearrange("p (a b) -> p a b", a=4)
                    dst = xnT3[:, 4 * gidx:4 * gidx + 4, tt * 128:(tt + 1) * 128]
                    v.tensor_copy(dst, src).then_inc(SM["vTr"], 1)
            for tt in range(NT):
                p = tt % 2
                if tt == 0:
                    v.wait_ge(SM["tLg"], NT)
                v.reduce_max(negA[:, tt:tt + 1], lgp_of(tt), axis=AX.X, negate=True
                             ).then_inc(SM["vLa"], 1)
                v.wait_ge(SM["sLx"], tt + 1)
                vi(v.reciprocal(inv[:, :], sume[p][:, :]))
                v.wait_ge(SM["vW"], vw_count[0])
                vi(v.tensor_scalar_mul(prb[p][:, :], prb[p][:, :], inv[:, :]))
                v.wait_ge(SM["vW"], vw_count[0])
                vi(v.reduce_max(negb[:, :], prb[p][:, :], axis=AX.X, negate=True))
                v.wait_ge(SM["vW"], vw_count[0])
                vi(v.tensor_scalar_mul(negb[:, :], negb[:, :], -1.0))
                v.wait_ge(SM["vW"], vw_count[0])
                vi(v.tensor_single_scalar(lgw[:, :], prb[p][:, :], negb[:, :], ALU.is_ge))
                v.wait_ge(SM["vW"], vw_count[0])
                vi(v.tensor_sub(lgw[:, :], prb[p][:, :], lgw[:, :]))
                v.wait_ge(SM["vW"], vw_count[0])
                vi(v.reduce_max(negb[:, :], lgw[:, :], axis=AX.X, negate=True))
                v.wait_ge(SM["vW"], vw_count[0])
                vi(v.tensor_scalar_mul(negb[:, :], negb[:, :], -1.0))
                v.wait_ge(SM["vW"], vw_count[0])
                v.scalar_tensor_tensor(cmb[p][:, :], prb[p][:, :], negb[:, :],
                                       prb[p][:, :], ALU.is_ge, ALU.mult
                                       ).then_inc(SM["vCb"], 1)
                v.wait_ge(SM["tCT"], tt + 1)
                v.tensor_copy(combT[:, tt * 128:(tt + 1) * 128], ctp_of(p)[0:8, :]
                              ).then_inc(SM["vCT"], 1)
            for r in range(NR):
                if r == E:
                    v.memset(comb_bc[:, :], 1.0).then_inc(SM["vCbc"], 1)
                for hc in range(NHC):
                    for tg in range(4):
                        G = (r * NHC + hc) * 4 + tg
                        v.wait_ge(SM["tGb"], G + 1)
                        if r < E:
                            DS["dCb"].wait(v, ("cb", r))
                        else:
                            v.wait_ge(SM["vCbc"], 1)
                        v.tensor_tensor(upb[:, tg * 512:(tg + 1) * 512], ups[G % 2],
                                        comb_bc[:, tg * 512:(tg + 1) * 512], ALU.mult
                                        ).then_inc(SM["vT1"], 1)
                        v.wait_ge(SM["sSl"], G + 1)
                        v.wait_ge(SM["vT1"], G + 1)
                        v.tensor_tensor(
                            gpr[:, hc * 2048 + tg * 512: hc * 2048 + (tg + 1) * 512],
                            sgb[:, tg * 512:(tg + 1) * 512],
                            upb[:, tg * 512:(tg + 1) * 512], ALU.mult
                        ).then_inc(SM["vT2"], 1)
                for dc in range(ND):
                    for tg in range(4):
                        u = r * 64 + dc * 4 + tg
                        v.wait_ge(SM["tDn"], u + 1)
                        if u >= 4:
                            DS["dRw"].wait(v, ("rw", u - 4))
                        if r == 0:
                            v.tensor_copy(outb[u % 4][:, :], dps[u % 2]).then_inc(SM["vDn"], 1)
                        else:
                            DS["dRr"].wait(v, ("rr", u))
                            v.tensor_tensor(outb[u % 4][:, :], dps[u % 2], rmwb[u % 3][:, :],
                                            ALU.add).then_inc(SM["vDn"], 1)

        @blk.scalar
        def _(s):
            def sq(tt):
                p = tt % 2
                DS["dX"].wait(s, ("x", tt))
                if tt >= 2:
                    s.wait_ge(SM["tTr"], 4 * (tt - 1))
                    s.wait_ge(SM["vSt"], tt - 1)
                s.activation(xnm[p][:, :], xt[p][:, :], AF.Square,
                             accum_out=ssq[p][:, :]).then_inc(SM["sSq"], 1)

            sq(0)
            for tt in range(NT):
                p = tt % 2
                if tt + 1 < NT:
                    sq(tt + 1)
                s.wait_ge(SM["vVar"], tt + 1)
                s.activation(sd[p][:, :], stat[p][:, 4:5], AF.Sqrt).then_inc(SM["sSd"], 1)
                s.wait_ge(SM["vSt"], tt + 1)
                s.activation(xnm[p][:, :], xt[p][:, :], AF.Identity,
                             bias=stat[p][:, 6:7], scale=stat[p][:, 5:6]
                             ).then_inc(SM["sXn"], 1)
            for tt in range(NT):
                p = tt % 2
                s.wait_ge(SM["vLa"], tt + 1)
                if tt >= 2:
                    s.wait_ge(SM["vCb"], tt - 1)
                s.activation(prb[p][:, :], lgp_of(tt), AF.Exp, bias=negA[:, tt:tt + 1],
                             accum_out=sume[p][:, :]).then_inc(SM["sLx"], 1)
            for r in range(NR):
                for hc in range(NHC):
                    for tg in range(4):
                        G = (r * NHC + hc) * 4 + tg
                        s.wait_ge(SM["tGa"], G + 1)
                        if G >= 4:
                            s.wait_ge(SM["vT2"], G - 3)
                        s.activation(sgb[:, tg * 512:(tg + 1) * 512], gps[G % 2], AF.Silu
                                     ).then_inc(SM["sSl"], 1)

        @blk.tensor
        def _(t):
            t.wait_ge(SM["vI"], 1)
            for tt in range(NT):
                p = tt % 2
                for gidx in range(4):
                    G = 4 * tt + gidx
                    if gidx == 0:
                        t.wait_ge(SM["sXn"], tt + 1)
                    if G >= 2:
                        t.wait_ge(SM["vTr"], G - 1)
                    trp = trp_of(G)
                    for j in range(4):
                        inst = t.transpose(
                            trp[:, j * 128:(j + 1) * 128],
                            xnm[tt % 2][:, (4 * gidx + j) * 128:(4 * gidx + j + 1) * 128],
                            ident[:, :])
                    inst.then_inc(SM["tTr"], 1)
                if tt == 0:
                    DS["dG"].wait(t, "setup")
                t.wait_ge(SM["vTr"], 4 * (tt + 1))
                for dc in range(ND):
                    inst = t.matmul(lgp_of(tt), xnT3[:, dc, tt * 128:(tt + 1) * 128],
                                    gwv[:, dc * 8:(dc + 1) * 8],
                                    start=(dc == 0), stop=(dc == ND - 1))
                inst.then_inc(SM["tLg"], 1)
            for tt in range(NT):
                p = tt % 2
                t.wait_ge(SM["vCb"], tt + 1)
                if tt >= 2:
                    t.wait_ge(SM["vCT"], tt - 1)
                t.transpose(ctp_of(p)[0:8, :], cmb[p][:, :], ident[:, :]).then_inc(SM["tCT"], 1)
            for r in range(NR):
                for hc in range(NHC):
                    c = r * NHC + hc
                    for tg in range(4):
                        G = c * 4 + tg
                        if tg == 0:
                            DS["dWg"].wait(t, ("gu", c))
                        if G >= 2:
                            t.wait_ge(SM["sSl"], G - 1)
                        for dc in range(ND):
                            inst = t.matmul(gps[G % 2], gw[c % 2][:, dc * 128:(dc + 1) * 128],
                                            xnT3[:, dc, tg * 512:(tg + 1) * 512],
                                            start=(dc == 0), stop=(dc == ND - 1))
                        inst.then_inc(SM["tGa"], 1)
                        if G >= 2:
                            t.wait_ge(SM["vT1"], G - 1)
                        for dc in range(ND):
                            inst = t.matmul(ups[G % 2], uw[c % 2][:, dc * 128:(dc + 1) * 128],
                                            xnT3[:, dc, tg * 512:(tg + 1) * 512],
                                            start=(dc == 0), stop=(dc == ND - 1))
                        inst.then_inc(SM["tGb"], 1)
                for dc in range(ND):
                    for tg in range(4):
                        u = r * 64 + dc * 4 + tg
                        if dc == 0 and tg == 0:
                            DS["dWd"].wait(t, ("wd", r))
                            t.wait_ge(SM["vT2"], 44 * (r + 1))
                        if u >= 2:
                            t.wait_ge(SM["vDn"], u - 1)
                        for hc in range(NHC):
                            inst = t.matmul(
                                dps[u % 2],
                                wdb[:, hc * 2048 + dc * 128: hc * 2048 + (dc + 1) * 128],
                                gpr[:, hc * 2048 + tg * 512: hc * 2048 + (tg + 1) * 512],
                                start=(hc == 0), stop=(hc == NHC - 1))
                        inst.then_inc(SM["tDn"], 1)

    st.close()

    st = ExitStack()
    ckv = st.enter_context(nc.sbuf_tensor("ckv", [128, NCC * S], BF16, side="right"))
    cq = st.enter_context(nc.sbuf_tensor("cq", [128, NQC * S], BF16, side="right"))
    st3 = ExitStack()
    dkvw = [st3.enter_context(nc.sbuf_tensor(f"dkvw{p}", [128, 2048], BF16, side="right"))
            for p in range(2)]
    dqw = [st3.enter_context(nc.sbuf_tensor(f"dqw{p}", [128, 2048], BF16, side="right"))
           for p in range(2)]

    ckv3 = ckv.ap().rearrange("p (cc t) -> p cc t", cc=NCC)
    cq3 = cq.ap().rearrange("p (qc t) -> p qc t", qc=NQC)
    pjp = [B[:, 1024:1536], B[:, 1536:2048]]
    wotA = nc.alloc_sbuf_tensor_at("wotA", [128, 12 * 2048], BF16,
                                   offset=nc.lookup_mloc(cq).addr)
    wotB = nc.alloc_sbuf_tensor_at("wotB", [128, 4 * 2048], BF16,
                                   offset=nc.lookup_mloc(ckv).addr)

    with nc.Block() as blk:
        @blk.gpsimd
        def _(g):
            for cc in range(NCC):
                if cc >= 2:
                    g.wait_ge(SM["tPj"], 4 * (cc - 1))
                DS["dPw"].issue(g.dma_start(dkvw[cc % 2][:, :], dkvT.ap()[cc]),
                                cc, ("dkv", cc))
            for qc in range(NQC):
                if qc >= 2:
                    g.wait_ge(SM["tPj"], 16 + 4 * (qc - 1))
                DS["dQw"].issue(g.dma_start(dqw[qc % 2][:, :], dqT.ap()[qc]),
                                qc, ("dq", qc))

        @blk.tensor
        def _(t):
            for cc in range(NCC):
                for tg in range(4):
                    j = cc * 4 + tg
                    if tg == 0:
                        DS["dPw"].wait(t, ("dkv", cc))
                    if j >= 2:
                        t.wait_ge(SM["sPj"], j - 1)
                    for dc in range(ND):
                        inst = t.matmul(pjp[j % 2], dkvw[cc % 2][:, dc * 128:(dc + 1) * 128],
                                        xnT3[:, dc, tg * 512:(tg + 1) * 512],
                                        start=(dc == 0), stop=(dc == ND - 1))
                    inst.then_inc(SM["tPj"], 1)
            for qc in range(NQC):
                for tg in range(4):
                    j = 16 + qc * 4 + tg
                    if tg == 0:
                        DS["dQw"].wait(t, ("dq", qc))
                    t.wait_ge(SM["sPj"], j - 1)
                    for dc in range(ND):
                        inst = t.matmul(pjp[j % 2], dqw[qc % 2][:, dc * 128:(dc + 1) * 128],
                                        xnT3[:, dc, tg * 512:(tg + 1) * 512],
                                        start=(dc == 0), stop=(dc == ND - 1))
                    inst.then_inc(SM["tPj"], 1)

        @blk.scalar
        def _(s):
            DS["dB"].wait(s, "bias")
            for cc in range(NCC):
                for tg in range(4):
                    j = cc * 4 + tg
                    s.wait_ge(SM["tPj"], j + 1)
                    s.activation(ckv3[:, cc, tg * 512:(tg + 1) * 512], pjp[j % 2], AF.Identity,
                                 bias=b_dkv[:, cc:cc + 1]).then_inc(SM["sPj"], 1)
            for qc in range(NQC):
                for tg in range(4):
                    j = 16 + qc * 4 + tg
                    s.wait_ge(SM["tPj"], j + 1)
                    s.activation(cq3[:, qc, tg * 512:(tg + 1) * 512], pjp[j % 2], AF.Identity,
                                 bias=b_dq[:, qc:qc + 1]).then_inc(SM["sPj"], 1)

    st3.close()

    kh = st.enter_context(nc.sbuf_tensor("kh", [128, S], BF16, side="right"))
    qh = st.enter_context(nc.sbuf_tensor("qh", [128, S], BF16, side="right"))
    vh = [st.enter_context(nc.sbuf_tensor(f"vh{p}", [128, NT * 132], BF16, side="right"))
          for p in range(2)]
    atm = st.enter_context(nc.sbuf_tensor("atm", [128, S], BF16, side="right"))
    ukw = [st.enter_context(nc.sbuf_tensor("ukw0", [128, 512], BF16, side="right"))]
    uvw = [st.enter_context(nc.sbuf_tensor("uvw0", [128, 512], BF16, side="right"))]
    uqw = [st.enter_context(nc.sbuf_tensor("uqw0", [128, 1536], BF16, side="right"))]
    pt = [st.enter_context(nc.sbuf_tensor(f"pt{p}", [128, 16 * 512], BF16, side="right"))
          for p in range(3)]

    vh3 = [vv.ap().rearrange("p (kt c) -> p kt c", kt=NT) for vv in vh]
    avp = [B[:, 0:512], B[:, 512:1024], B[:, 1024:1536], B[:, 1536:2048]]
    attnT = xnT
    attnT3 = xnT3
    pj4 = [A[:, 1024:1536], A[:, 1536:2048]]

    def trp4_of(gt):
        return (A[:, 0:256] if gt % 2 == 0 else A[:, 512:768]).bitcast(BF16)

    def proj_consumer(ju):
        h, idx = ju // 24, ju % 24
        if idx < 4:
            return ("sKQ", 8 * h + idx + 1)
        if idx < 8:
            return ("sKQ", 8 * h + idx + 1)
        return ("vV", 16 * h + (idx - 8) + 1)

    with nc.Block() as blk:
        @blk.gpsimd
        def _(g):
            g.memset(vh3[0][:, :, 128:129], 1.0).then_inc(SM["gOn"], 1)
            g.memset(vh3[1][:, :, 128:129], 1.0).then_inc(SM["gOn"], 1)
            for h in range(H):
                if h == 0:
                    g.wait_ge(SM["tPj"], 64)
                if h >= 1:
                    g.wait_ge(SM["tKQ"], 24 * (h - 1) + 4)
                DS["dHk"].issue(g.dma_start(ukw[0][:, :], ukT.ap()[h]), 0, ("h", h))
                if h >= 1:
                    g.wait_ge(SM["tKQ"], 24 * (h - 1) + 8)
                DS["dHq"].issue(g.dma_start(uqw[0][:, :], uqT.ap()[h]), 0, ("h", h))
                if h >= 1:
                    g.wait_ge(SM["tKQ"], 24 * h)
                DS["dHv"].issue(g.dma_start(uvw[0][:, :], uvT.ap()[h]), 0, ("h", h))
            g.wait_ge(SM["tKQ"], 24 * H)
            for hc in range(ND):
                dst = wotA[:, hc * 2048:(hc + 1) * 2048] if hc < 12 else \
                    wotB[:, (hc - 12) * 2048:(hc - 12 + 1) * 2048]
                DS["dWo"].issue(g.dma_start(dst, woT.ap()[hc]), hc, ("wo", hc))

        @blk.tensor
        def _(t):
            for h in range(H):
                if h == 0:
                    t.wait_ge(SM["gOn"], 2)
                    t.wait_ge(SM["sPj"], 64)
                for idx in range(24):
                    ju = 24 * h + idx
                    if idx == 0:
                        DS["dHk"].wait(t, ("h", h))
                    elif idx == 4:
                        DS["dHq"].wait(t, ("h", h))
                    elif idx == 8:
                        DS["dHv"].wait(t, ("h", h))
                    if idx == 0 and h >= 1:
                        t.wait_ge(SM["sEx"], 16 * h)
                    if ju >= 2:
                        sem, val = proj_consumer(ju - 2)
                        t.wait_ge(SM[sem], val)
                    pp = pj4[ju % 2]
                    if idx < 4:
                        tg = idx
                        for cc in range(NCC):
                            inst = t.matmul(pp, ukw[0][:, cc * 128:(cc + 1) * 128],
                                            ckv3[:, cc, tg * 512:(tg + 1) * 512],
                                            start=(cc == 0), stop=(cc == NCC - 1))
                    elif idx < 8:
                        tg = idx - 4
                        for qc in range(NQC):
                            inst = t.matmul(pp, uqw[0][:, qc * 128:(qc + 1) * 128],
                                            cq3[:, qc, tg * 512:(tg + 1) * 512],
                                            start=(qc == 0), stop=(qc == NQC - 1))
                    else:
                        kt = idx - 8
                        for cc in range(NCC):
                            inst = t.matmul(pp[:, 0:128],
                                            ckv3[:, cc, kt * 128:(kt + 1) * 128],
                                            uvw[0][:, cc * 128:(cc + 1) * 128],
                                            start=(cc == 0), stop=(cc == NCC - 1))
                    inst.then_inc(SM["tKQ"], 1)
                def av_group(gq, qsub):
                    hh, qgg = gq // 4, gq % 4
                    q16 = gq * 4 + qsub
                    if qsub == 0:
                        t.wait_ge(SM["sEx"], gq * 4 + 4)
                        if qgg == 0:
                            t.wait_ge(SM["vV"], 16 * (hh + 1))
                    if q16 >= 4:
                        t.wait_ge(SM["vAv"], 2 * (q16 - 3))
                    for kt in range(NT):
                        inst = t.matmul(
                            avp[q16 % 4][:, 0:129],
                            pt[gq % 3][:, kt * 512 + qsub * 128:
                                       kt * 512 + (qsub + 1) * 128],
                            vh3[hh % 2][:, kt, 0:129],
                            start=(kt == 0), stop=(kt == NT - 1))
                    inst.then_inc(SM["tAV"], 1)

                def transposes_of(hh, sex_done):
                    t.wait_ge(SM["sEx"], sex_done)
                    for gidx in range(4):
                        gt = 4 * hh + gidx
                        t.wait_ge(SM["vAv"], 2 * (hh * 16 + 4 * (gidx + 1)))
                        if gt >= 2:
                            t.wait_ge(SM["vAT"], gt - 1)
                        trp = trp4_of(gt)
                        for j in range(4):
                            inst = t.transpose(
                                trp[:, j * 128:(j + 1) * 128],
                                atm[:, (4 * gidx + j) * 128:(4 * gidx + j + 1) * 128],
                                ident[:, :])
                        inst.then_inc(SM["tAT"], 1)

                for qg in range(4):
                    gq = h * 4 + qg
                    base16 = gq * 16
                    for kt in range(NT):
                        if kt == 0 and qg == 0:
                            t.wait_ge(SM["sKQ"], 8 * (h + 1))
                        if kt == 0 and qg == 1 and h >= 1:
                            t.wait_ge(SM["vAT"], 4 * h)
                        if qg == 0 and kt == 2:
                            t.wait_ge(SM["vV"], 16 * h + 15)
                        if qg == 0 and kt == 3:
                            t.wait_ge(SM["vV"], 16 * h + 16)
                        if base16 + kt >= 4:
                            gl = base16 + kt - 4
                            t.wait_ge(SM["sEx"], gl // 4 + 1)
                        t.matmul(A[:, (kt % 4) * 512:(kt % 4 + 1) * 512],
                                 kh[:, kt * 128:(kt + 1) * 128],
                                 qh[:, qg * 512:(qg + 1) * 512],
                                 start=True, stop=True).then_inc(SM["tS"], 1)
                        if kt % 4 == 3 and gq >= 1:
                            av_group(gq - 1, kt // 4)
                            if gq % 4 == 0 and kt == 15:
                                transposes_of(h - 1, 4 * (4 * h) + 4)
                if h == H - 1:
                    for qsub in range(4):
                        av_group(4 * H - 1, qsub)
                    transposes_of(H - 1, 16 * H)

        @blk.scalar
        def _(s):
            DS["dB"].wait(s, "bias")
            for h in range(H):
                for qg in range(4):
                    gq = h * 4 + qg
                    base16 = gq * 16
                    for j in range(4):
                        s.wait_ge(SM["tS"], base16 + 4 * j + 4)
                        if j == 0 and gq >= 3:
                            s.wait_ge(SM["tAV"], (gq - 2) * 4)
                        s.activation(pt[gq % 3][:, (4 * j) * 512:(4 * j + 4) * 512],
                                     A[:, 0:2048],
                                     AF.Exp, scale=SCALE).then_inc(SM["sEx"], 1)

        @blk.vector
        def _(v):
            DS["dG"].wait(v, "setup")

            def attn_evacs(hh, qlo, qhi):
                for qs in range(qlo, qhi):
                    q16 = hh * 16 + qs
                    v.wait_ge(SM["tAV"], q16 + 1)
                    if q16 >= 1:
                        v.wait_ge(SM["vAv"], 2 * q16)
                    vi(v.reciprocal(inv[:, :], avp[q16 % 4][:, 128:129]))
                    v.wait_ge(SM["vW"], vw_count[0])
                    v.tensor_scalar_mul(atm[:, qs * 128:(qs + 1) * 128],
                                        avp[q16 % 4][:, 0:128], inv[:, :]
                                        ).then_inc(SM["vAv"], 2)

            def attnT_evacs(hh):
                for gidx in range(4):
                    gt = 4 * hh + gidx
                    v.wait_ge(SM["tAT"], gt + 1)
                    v.tensor_copy(attnT3[:, hh, gidx * 512:(gidx + 1) * 512], trp4_of(gt)
                                  ).then_inc(SM["vAT"], 1)

            for h in range(H):
                for idx in range(8):
                    ju = 24 * h + idx
                    v.wait_ge(SM["tKQ"], ju + 1)
                    if idx < 4:
                        v.tensor_scalar_add(kh[:, idx * 512:(idx + 1) * 512], pj4[ju % 2],
                                            b_uk[:, h:h + 1]).then_inc(SM["sKQ"], 1)
                    else:
                        tg = idx - 4
                        v.tensor_scalar_add(qh[:, tg * 512:(tg + 1) * 512], pj4[ju % 2],
                                            b_uq[:, h:h + 1]).then_inc(SM["sKQ"], 1)
                for kt in range(NT):
                    ju = 24 * h + 8 + kt
                    if kt == 0 and h >= 2:
                        v.wait_ge(SM["tAV"], 16 * (h - 1))
                    v.wait_ge(SM["tKQ"], ju + 1)
                    v.tensor_tensor(vh3[h % 2][:, kt, 0:128], pj4[ju % 2][:, 0:128],
                                    uvb_bc[:, h * 128:(h + 1) * 128], ALU.add
                                    ).then_inc(SM["vV"], 1)
                if h >= 1:
                    attn_evacs(h - 1, 12, 16)
                    attnT_evacs(h - 1)
                attn_evacs(h, 0, 12)
            attn_evacs(H - 1, 12, 16)
            attnT_evacs(H - 1)

    st.close()

    st = ExitStack()
    prt = [st.enter_context(nc.sbuf_tensor(f"prt{p}", [128, 512], F32, side="left"))
           for p in range(2)]
    xtb = [st.enter_context(nc.sbuf_tensor(f"xtb{p}", [128, 512], F32, side="left"))
           for p in range(2)]
    tmpb = [st.enter_context(nc.sbuf_tensor("tmpb0", [128, 512], F32, side="left"))]
    fout = [st.enter_context(nc.sbuf_tensor(f"fout{p}", [128, 512], F32, side="left"))
            for p in range(2)]

    def wot_tile(hc, oc):
        if hc < 12:
            return wotA[:, hc * 2048 + oc * 128: hc * 2048 + (oc + 1) * 128]
        return wotB[:, (hc - 12) * 2048 + oc * 128: (hc - 12) * 2048 + (oc + 1) * 128]

    wops = [A[:, 0:512], A[:, 512:1024], A[:, 1024:1536], A[:, 1536:2048]]

    with nc.Block() as blk:
        @blk.sync
        def _(sy):
            for uu in range(NR * 64 - 4, NR * 64):
                DS["dRw"].wait(sy, ("rw", uu))
            for j in range(-2, 64):
                rd = j + 2
                if rd < 64:
                    oc, tg = rd // 4, rd % 4
                    if rd >= 2:
                        sy.wait_ge(SM["vWo"], 2 * (rd - 2) + 2)
                    DS["dPr"].issue(
                        sy.dma_start(prt[rd % 2][:, :],
                                     partial.ap()[oc * 128:(oc + 1) * 128,
                                                  tg * 512:(tg + 1) * 512]),
                        rd, ("pr", rd))
                    DS["dXr"].issue(
                        sy.dma_start(xtb[rd % 2][:, :],
                                     xTd.ap()[oc * 128:(oc + 1) * 128,
                                              tg * 512:(tg + 1) * 512]),
                        rd, ("xr", rd))
                if j >= 0:
                    oc, tg = j // 4, j % 4
                    sy.wait_ge(SM["vWo"], 2 * j + 2)
                    DS["dOw"].issue(
                        sy.dma_start(outT.ap()[oc * 128:(oc + 1) * 128,
                                               tg * 512:(tg + 1) * 512],
                                     fout[j % 2][:, :]),
                        j, ("ow", j))
            DS["dOw"].wait(sy, ("ow", 62))
            DS["dOw"].wait(sy, ("ow", 63))

        @blk.tensor
        def _(t):
            t.wait_ge(SM["vAT"], 4 * H)
            for j in range(64):
                oc, tg = j // 4, j % 4
                if j >= 4:
                    t.wait_ge(SM["vWo"], 2 * (j - 4) + 2)
                for hc in range(ND):
                    if j == 0:
                        DS["dWo"].wait(t, ("wo", hc))
                    elif j == 1 and hc == 0:
                        DS["dWo"].wait(t, ("wo", ND - 1))
                    inst = t.matmul(wops[j % 4], wot_tile(hc, oc),
                                    attnT3[:, hc, tg * 512:(tg + 1) * 512],
                                    start=(hc == 0), stop=(hc == ND - 1))
                inst.then_inc(SM["tWo"], 1)

        @blk.vector
        def _(v):
            DS["dB"].wait(v, "bias")
            for j in range(64):
                oc = j // 4
                v.wait_ge(SM["tWo"], j + 1)
                DS["dPr"].wait(v, ("pr", j))
                DS["dXr"].wait(v, ("xr", j))
                if j >= 1:
                    v.wait_ge(SM["vWo"], 2 * j)
                vi(v.scalar_tensor_tensor(tmpb[0][:, :], wops[j % 4], b_wo[:, oc:oc + 1],
                                          prt[j % 2][:, :], ALU.add, ALU.add))
                if j >= 2:
                    DS["dOw"].wait(v, ("ow", j - 2))
                v.wait_ge(SM["vW"], vw_count[0])
                v.tensor_add(fout[j % 2][:, :], tmpb[0][:, :], xtb[j % 2][:, :]
                             ).then_inc(SM["vWo"], 2)

    st.close()
    return nc



def _prep_weights(inputs):
    f32 = np.float32

    def C(a):
        return np.ascontiguousarray(a, dtype=f32)

    W = {}
    W["dkvT"] = C(inputs["L0_attn_W_DKV_w"].reshape(NCC, 128, ND, 128).transpose(0, 3, 2, 1)
                  .reshape(NCC, 128, 2048))
    W["dqT"] = C(inputs["L0_attn_W_DQ_w"].reshape(NQC, 128, ND, 128).transpose(0, 3, 2, 1)
                 .reshape(NQC, 128, 2048))
    W["ukT"] = C(inputs["L0_attn_W_UK_w"].reshape(H, 128, NCC, 128).transpose(0, 3, 2, 1)
                 .reshape(H, 128, 512))
    W["uvT"] = C(inputs["L0_attn_W_UV_w"].reshape(H, 128, NCC, 128).transpose(0, 3, 2, 1)
                 .reshape(H, 128, 512))
    W["uqT"] = C(inputs["L0_attn_W_UQ_w"].reshape(H, 128, NQC, 128).transpose(0, 3, 2, 1)
                 .reshape(H, 128, 1536))
    W["woT"] = C(inputs["L0_attn_W_O_w"].reshape(ND, 128, ND, 128).transpose(2, 3, 0, 1)
                 .reshape(ND, 128, 2048))
    W["gwp"] = C(inputs["L0_ffn_gate_w"].reshape(ND, 128, 8).transpose(1, 0, 2)
                 .reshape(128, 128))
    W["wgp"] = C(inputs["L0_ffn_w_gate"].reshape(E, ND, 128, NHC, 128)
                 .transpose(0, 3, 2, 1, 4).reshape(E, NHC, 128, 2048))
    W["wup"] = C(inputs["L0_ffn_w_up"].reshape(E, ND, 128, NHC, 128)
                 .transpose(0, 3, 2, 1, 4).reshape(E, NHC, 128, 2048))
    W["wdp"] = C(inputs["L0_ffn_w_down"].reshape(E, NHC, 128, 2048))
    W["swgp"] = C(inputs["L0_ffn_sw_gate"].reshape(ND, 128, 22, 128).transpose(2, 1, 0, 3)
                  .reshape(22, 128, 2048))
    W["swup"] = C(inputs["L0_ffn_sw_up"].reshape(ND, 128, 22, 128).transpose(2, 1, 0, 3)
                  .reshape(22, 128, 2048))
    W["swdp"] = C(inputs["L0_ffn_sw_down"].reshape(22, 128, 2048))
    W["dkvb"] = C(inputs["L0_attn_W_DKV_b"].reshape(NCC, 128).T)
    W["dqb"] = C(inputs["L0_attn_W_DQ_b"].reshape(NQC, 128).T)
    W["ukb"] = C(inputs["L0_attn_W_UK_b"].reshape(H, 128).T)
    W["uqb"] = C(inputs["L0_attn_W_UQ_b"].reshape(H, 128).T)
    W["wob"] = C(inputs["L0_attn_W_O_b"].reshape(ND, 128).T)
    W["uvbr"] = C(inputs["L0_attn_W_UV_b"].reshape(1, 2048))
    return W


def kernel(**inputs):
    x = np.asarray(inputs["L0_attn_hidden_states"], dtype=np.float32)
    W = _prep_weights(inputs)
    nc = build_nc()
    in_maps = []
    for c in range(NCORES):
        m = dict(W)
        m["xin"] = np.ascontiguousarray(x[c])
        m["xTd"] = np.ascontiguousarray(x[c].T)
        in_maps.append(m)
    res = run_bass_kernel_spmd(nc, in_maps, core_ids=list(range(NCORES)))
    out = np.stack([np.asarray(res.results[c]["outT"], dtype=np.float32).T
                    for c in range(NCORES)], axis=0)
    return np.ascontiguousarray(out)
